# version 02
# baseline (speedup 1.0000x reference)
import numpy as np
import ml_dtypes
from contextlib import ExitStack

import concourse.bass as bass
import concourse.mybir as mybir
from concourse.bass_utils import run_bass_kernel_spmd

F32 = mybir.dt.float32
BF16 = mybir.dt.bfloat16
I32 = mybir.dt.int32
AF = mybir.ActivationFunctionType
ALU = mybir.AluOpType
AX = mybir.AxisListType

NCORE = 8
T = 2048
NT = T // 128
D = 1024
RND = 12582912.0
LN_EPS = 1e-5


def _bf16(x):
    return np.ascontiguousarray(np.asarray(x, dtype=np.float32).astype(ml_dtypes.bfloat16))


def _f32(x):
    return np.ascontiguousarray(np.asarray(x), dtype=np.float32)


def _key(x):
    if isinstance(x, str):
        return x
    n = getattr(x, "name", None)
    if n is None:
        raise ValueError(f"no name for {x!r}")
    return n() if callable(n) else n


class Sched:
    ENGINES = ("sync", "vector", "scalar", "tensor", "gpsimd")

    def __init__(self, nc, ctx, serialize_dma=False):
        self.nc = nc
        self.serialize_dma = serialize_dma
        self.sems = {e: ctx.enter_context(nc.semaphore("sem_" + e))
                     for e in self.ENGINES}
        self.ops = {e: [] for e in self.ENGINES}
        self.count = {e: 0 for e in self.ENGINES}
        self.last_w = {}
        self.readers = {}

    def add(self, eng, fn, r=(), w=(), deps=(), dma=False):
        toks = list(deps)
        if dma and self.serialize_dma and self.count[eng] > 0:
            toks.append((eng, self.count[eng]))
        rkeys = [_key(x) for x in r]
        wkeys = [_key(x) for x in w]
        for k in rkeys:
            toks.append(self.last_w.get(k))
        for k in wkeys:
            toks.append(self.last_w.get(k))
            toks.extend(self.readers.get(k, ()))
        waits = {}
        for tok in toks:
            if tok is None:
                continue
            de, dv = tok
            if de == eng and eng == "tensor":
                continue
            waits[de] = max(waits.get(de, 0), dv)
        inc = 16 if dma else 1
        self.ops[eng].append((fn, waits, inc))
        self.count[eng] += inc
        tok = (eng, self.count[eng])
        for k in rkeys:
            self.readers.setdefault(k, []).append(tok)
        for k in wkeys:
            self.last_w[k] = tok
            self.readers[k] = []
        return tok

    def barrier_token(self):
        return [(e, c) for e, c in self.count.items() if c > 0]


def make_block(nc, ctx, sch, final_eng="gpsimd"):
    block = ctx.enter_context(nc.Block())
    counts = dict(sch.count)
    sems = sch.sems
    for eng in Sched.ENGINES:
        ops = sch.ops[eng]
        if not ops and eng != final_eng:
            continue

        def body(handle, _ops=list(ops), _eng=eng):
            g = {}
            for fn, waits, inc in _ops:
                for de, dv in sorted(waits.items()):
                    if g.get(de, 0) < dv:
                        handle.wait_ge(sems[de], dv)
                        g[de] = dv
                instr = fn(handle)
                instr.then_inc(sems[_eng], inc)
            if _eng == final_eng:
                for de, dv in sorted(counts.items()):
                    if de != _eng and dv > 0 and g.get(de, 0) < dv:
                        handle.wait_ge(sems[de], dv)
        getattr(block, eng)(body)
    return block



def emit_ln(sch, xin_ap, xin_buf, out_ap, out_buf, scratch, cols, eps_col_ap,
            colbase=0):
    A = sch.add
    nfeat = xin_ap.shape[-1]
    cS, cM, cQ, cV, cMM, cR = [f"lncol{colbase}+{j}" for j in range(6)]
    col = lambda j: cols[:, colbase + j:colbase + j + 1]
    A("vector", lambda v: v.reduce_sum(out=col(0), in_=xin_ap, axis=AX.X),
      r=[xin_buf], w=[cS])
    A("vector", lambda v: v.tensor_scalar(out=col(1), in0=col(0),
      scalar1=1.0 / nfeat, scalar2=None, op0=ALU.mult), r=[cS], w=[cM])
    A("scalar", lambda s: s.activation(out=scratch, in_=xin_ap, func=AF.Square),
      r=[xin_buf], w=[_key(scratch)])
    A("vector", lambda v: v.reduce_sum(out=col(2), in_=scratch, axis=AX.X),
      r=[_key(scratch)], w=[cQ])
    A("vector", lambda v: v.tensor_scalar(out=col(2), in0=col(2),
      scalar1=1.0 / nfeat, scalar2=None, op0=ALU.mult), r=[cQ], w=[cQ])
    A("vector", lambda v: v.tensor_mul(out=col(4), in0=col(1), in1=col(1)),
      r=[cM], w=[cMM])
    A("vector", lambda v: v.tensor_sub(out=col(3), in0=col(2), in1=col(4)),
      r=[cQ, cMM], w=[cV])
    A("scalar", lambda s: s.activation(out=col(5), in_=col(3), func=AF.Ln,
      bias=eps_col_ap), r=[cV, "epscol"], w=[cR])
    A("scalar", lambda s: s.activation(out=col(5), in_=col(5), func=AF.Exp,
      scale=-0.5), r=[cR], w=[cR])
    return A("vector", lambda v: v.tensor_scalar(out=out_ap, in0=xin_ap,
             scalar1=col(1), scalar2=col(5), op0=ALU.subtract, op1=ALU.mult),
             r=[xin_buf, cM, cR], w=[out_buf])


def emit_actquant(sch, x_ap, x_buf, outq_ap, outq_buf, cols, colbase, qmax,
                  inv_dst_ap=None, inv_dst_buf=None):
    A = sch.add
    cA, cS = f"aqc{colbase}", f"aqc{colbase}+1"
    col = lambda j: cols[:, colbase + j:colbase + j + 1]
    A("vector", lambda v: v.tensor_reduce(out=col(0), in_=x_ap, axis=AX.X,
      op=ALU.max, apply_absolute_value=True), r=[x_buf], w=[cA])
    A("vector", lambda v: v.tensor_scalar(out=col(0), in0=col(0),
      scalar1=1e-5, scalar2=None, op0=ALU.max), r=[cA], w=[cA])
    if inv_dst_ap is not None:
        A("vector", lambda v: v.tensor_scalar(out=inv_dst_ap, in0=col(0),
          scalar1=1.0 / qmax, scalar2=None, op0=ALU.mult),
          r=[cA], w=[inv_dst_buf])
    A("vector", lambda v: v.reciprocal(out=col(1), in_=col(0)), r=[cA], w=[cS])
    A("vector", lambda v: v.tensor_scalar(out=col(1), in0=col(1),
      scalar1=float(qmax), scalar2=None, op0=ALU.mult), r=[cS], w=[cS])
    A("vector", lambda v: v.tensor_scalar(out=x_ap, in0=x_ap, scalar1=col(1),
      scalar2=None, op0=ALU.mult), r=[x_buf, cS], w=[x_buf])
    A("vector", lambda v: v.tensor_scalar(out=x_ap, in0=x_ap, scalar1=RND,
      scalar2=RND, op0=ALU.add, op1=ALU.subtract), r=[x_buf], w=[x_buf])
    return A("vector", lambda v: v.tensor_scalar(out=outq_ap, in0=x_ap,
             scalar1=float(-qmax - 1), scalar2=float(qmax),
             op0=ALU.max, op1=ALU.min), r=[x_buf], w=[outq_buf])


def emit_transpose(sch, src_ap, src_buf, dst_ap, dst_buf, ps_t, ident,
                   ps_key="ps_t", ident_key="ident"):
    A = sch.add
    A("tensor", lambda t: t.transpose(out=ps_t[:src_ap.shape[-1], :src_ap.shape[0]],
      in_=src_ap, identity=ident[:]), r=[src_buf, ident_key], w=[ps_key])
    return A("vector", lambda v: v.tensor_copy(
        out=dst_ap, in_=ps_t[:src_ap.shape[-1], :src_ap.shape[0]]),
        r=[ps_key], w=[dst_buf])


def emit_wquant_chunk(sch, dst_ap, dst_buf, stg_ap, stg_buf, scale_col_ap):
    A = sch.add
    A("vector", lambda v: v.tensor_scalar(out=stg_ap, in0=stg_ap,
      scalar1=scale_col_ap, scalar2=None, op0=ALU.mult),
      r=[stg_buf, "wcol"], w=[stg_buf])
    A("vector", lambda v: v.tensor_scalar(out=stg_ap, in0=stg_ap, scalar1=RND,
      scalar2=RND, op0=ALU.add, op1=ALU.subtract), r=[stg_buf], w=[stg_buf])
    return A("vector", lambda v: v.tensor_scalar(out=dst_ap, in0=stg_ap,
             scalar1=-1.0, scalar2=1.0, op0=ALU.max, op1=ALU.min),
             r=[stg_buf], w=[dst_buf])



def build_ffn0(debug=False):
    nc = bass.Bass("TRN2", target_bir_lowering=False, debug=False)
    x_d = nc.dram_tensor("x", [T, D], F32, kind="ExternalInput").ap()
    wgT_d = nc.dram_tensor("wgT", [D, 4096], F32, kind="ExternalInput").ap()
    wuT_d = nc.dram_tensor("wuT", [D, 4096], F32, kind="ExternalInput").ap()
    wdT_d = nc.dram_tensor("wdT", [4096, D], F32, kind="ExternalInput").ap()
    ident_d = nc.dram_tensor("ident", [128, 128], BF16, kind="ExternalInput").ap()
    onescol_d = nc.dram_tensor("onescol", [128, 1], F32, kind="ExternalInput").ap()
    onesrow_d = nc.dram_tensor("onesrow", [1, 128], F32, kind="ExternalInput").ap()
    xo_d = nc.dram_tensor("xo", [T, D], F32, kind="ExternalOutput").ap()
    h_d = nc.dram_tensor("h_scratch", [T, 4096], F32).ap()
    if debug:
        dbg_d = nc.dram_tensor("dbg", [128, 64 + 8 + 1024 + 1024], F32, kind="ExternalOutput").ap()

    with ExitStack() as ctx:
        e = ctx.enter_context
        ident = e(nc.sbuf_tensor("ident_s", [128, 128], BF16))
        onescol = e(nc.sbuf_tensor("onescol_s", [128, 1], F32))
        onesrow = e(nc.sbuf_tensor("onesrow_s", [1, 128], F32))
        xtile = e(nc.sbuf_tensor("xtile", [128, D], F32))
        ln = e(nc.sbuf_tensor("ln", [128, D], F32))
        lnsc = e(nc.sbuf_tensor("lnsc", [128, D], F32))
        xq = e(nc.sbuf_tensor("xq", [128, D], BF16))
        xqT = e(nc.sbuf_tensor("xqT", [128, NT * D], BF16))
        wstg = e(nc.sbuf_tensor("wstg", [128, 4096], F32))
        wstg2 = e(nc.sbuf_tensor("wstg2", [128, 4096], F32))
        wq1 = e(nc.sbuf_tensor("wq1", [128, 4096], BF16))
        wq2 = e(nc.sbuf_tensor("wq2", [128, 4096], BF16))
        wqd = e(nc.sbuf_tensor("wqd", [128, 32 * 1024], BF16))
        gate = e(nc.sbuf_tensor("gate", [128, 512], F32))
        up = e(nc.sbuf_tensor("up", [128, 512], F32))
        hbuf = e(nc.sbuf_tensor("hbuf", [128, 512], F32))
        hrow = e(nc.sbuf_tensor("hrow", [128, 4096], F32))
        hq = e(nc.sbuf_tensor("hq", [128, 4096], BF16))
        hqT = e(nc.sbuf_tensor("hqT", [128, 4096], BF16))
        outt = e(nc.sbuf_tensor("outt", [128, 512], F32))
        cols = e(nc.sbuf_tensor("cols", [128, 64], F32))
        wcol = e(nc.sbuf_tensor("wcol", [128, 8], F32))
        epscol = e(nc.sbuf_tensor("epscol", [128, 1], F32))
        sv = e(nc.sbuf_tensor("sv", [1, 16], F32))
        svb = e(nc.sbuf_tensor("svb", [1, 16], BF16))
        ps_g = e(nc.psum_tensor("ps_g", [128, 512], F32))
        ps_u = e(nc.psum_tensor("ps_u", [128, 512], F32))
        ps_d = e(nc.psum_tensor("ps_d", [128, 512], F32))
        ps_t = e(nc.psum_tensor("ps_t", [128, 128], BF16))
        ps_s = e(nc.psum_tensor("ps_s", [1, 512], F32))
        ps_c = e(nc.psum_tensor("ps_c", [128, 512], F32))

        sch = Sched(nc, ctx)
        A = sch.add

        A("sync", lambda s_: s_.dma_start(out=ident[:], in_=ident_d),
          w=["ident"], dma=True)
        A("sync", lambda s_: s_.dma_start(out=onescol[:], in_=onescol_d),
          w=["onescol"], dma=True)
        A("sync", lambda s_: s_.dma_start(out=onesrow[:], in_=onesrow_d),
          w=["onesrow"], dma=True)
        A("vector", lambda v: v.memset(epscol[:], LN_EPS), w=["epscol"])

        x_t = x_d.rearrange("(n p) d -> n p d", p=128)

        def absmean(w_d, nchunk, chunk_elems, c_acc):
            w_t = w_d.rearrange("(n p) d -> n p d", p=128)
            acc = f"absacc{c_acc}"
            tmp = f"abstmp{c_acc}"
            A("vector", lambda v: v.memset(cols[:, c_acc:c_acc + 1], 0.0), w=[acc])
            bufs = [(wstg, "wstg"), (wstg2, "wstg2")]
            for c in range(nchunk):
                b, bk = bufs[c % 2]
                A("sync", (lambda c, b: lambda s_: s_.dma_start(
                    out=b[:, :chunk_elems], in_=w_t[c]))(c, b), w=[bk], dma=True)
                A("vector", (lambda c, b: lambda v: v.tensor_reduce(
                    out=cols[:, c_acc + 1:c_acc + 2], in_=b[:, :chunk_elems],
                    axis=AX.X, op=ALU.add, apply_absolute_value=True))(c, b),
                    r=[bk], w=[tmp])
                A("vector", lambda v: v.tensor_add(
                    out=cols[:, c_acc:c_acc + 1], in0=cols[:, c_acc:c_acc + 1],
                    in1=cols[:, c_acc + 1:c_acc + 2]), r=[tmp, acc], w=[acc])
            return acc

        accs = [absmean(wgT_d, 8, 4096, 8), absmean(wuT_d, 8, 4096, 10),
                absmean(wdT_d, 32, 1024, 12)]
        A("tensor", lambda t: t.matmul(
            out=ps_s[0:1, 0:3], lhsT=onescol[:],
            rhs=cols[:, 8:13:2], start=True, stop=True),
            r=accs + ["onescol"], w=["ps_s0"])
        A("vector", lambda v: v.tensor_scalar(
            out=sv[0:1, 0:3], in0=ps_s[0:1, 0:3],
            scalar1=1.0 / (D * 4096), scalar2=1e-5, op0=ALU.mult, op1=ALU.max),
            r=["ps_s0"], w=["sv"])
        A("vector", lambda v: v.reciprocal(out=sv[0:1, 4:7], in_=sv[0:1, 0:3]),
          r=["sv"], w=["sv"])
        A("tensor", lambda t: t.matmul(
            out=ps_c[:, 0:8], lhsT=onesrow[0:1, :], rhs=sv[0:1, 0:8],
            start=True, stop=True), r=["sv", "onesrow"], w=["ps_c"])
        A("vector", lambda v: v.tensor_copy(out=wcol[:, 0:8], in_=ps_c[:, 0:8]),
          r=["ps_c"], w=["wcol"])

        for i in range(NT):
            A("sync", (lambda i: lambda s_: s_.dma_start(
                out=xtile[:], in_=x_t[i]))(i), w=["xtile"], dma=True)
            emit_ln(sch, xtile[:], "xtile", ln[:], "ln", lnsc[:], cols, epscol[:, 0:1])
            emit_actquant(sch, ln[:], "ln", xq[:], "xq", cols, 6, 127,
                          inv_dst_ap=cols[:, 32 + i:33 + i], inv_dst_buf=f"sxinv{i}")
            for c in range(8):
                emit_transpose(sch, xq[:, c * 128:(c + 1) * 128], "xq",
                               xqT[:, (i * 8 + c) * 128:(i * 8 + c + 1) * 128],
                               f"xqT{i}", ps_t, ident)

        h_t = h_d.rearrange("(n p) d -> n p d", p=128)
        for n in range(8):
            A("sync", (lambda n: lambda s_: s_.dma_start(
                out=wstg[:].rearrange("p (k f) -> p k f", f=512),
                in_=wgT_d[:, n * 512:(n + 1) * 512].rearrange("(k p) f -> p k f", p=128)))(n),
                w=["wstg"], dma=True)
            A("sync", (lambda n: lambda s_: s_.dma_start(
                out=wstg2[:].rearrange("p (k f) -> p k f", f=512),
                in_=wuT_d[:, n * 512:(n + 1) * 512].rearrange("(k p) f -> p k f", p=128)))(n),
                w=["wstg2"], dma=True)
            emit_wquant_chunk(sch, wq1[:], "wq1", wstg[:], "wstg", wcol[:, 4:5])
            emit_wquant_chunk(sch, wq2[:], "wq2", wstg2[:], "wstg2", wcol[:, 5:6])
            for i in range(NT):
                for c in range(8):
                    lhsT = xqT[:, (i * 8 + c) * 128:(i * 8 + c + 1) * 128]
                    A("tensor", (lambda c, lhsT: lambda t: t.matmul(
                        out=ps_g[:], lhsT=lhsT, rhs=wq1[:, c * 512:(c + 1) * 512],
                        start=(c == 0), stop=(c == 7)))(c, lhsT),
                        r=[f"xqT{i}", "wq1"], w=["ps_g"])
                    A("tensor", (lambda c, lhsT: lambda t: t.matmul(
                        out=ps_u[:], lhsT=lhsT, rhs=wq2[:, c * 512:(c + 1) * 512],
                        start=(c == 0), stop=(c == 7)))(c, lhsT),
                        r=[f"xqT{i}", "wq2"], w=["ps_u"])
                A("vector", (lambda i: lambda v: v.tensor_scalar(
                    out=cols[:, 16:17], in0=cols[:, 32 + i:33 + i],
                    scalar1=wcol[:, 0:1], scalar2=None, op0=ALU.mult))(i),
                    r=[f"sxinv{i}", "wcol"], w=["epig"])
                A("vector", (lambda i: lambda v: v.tensor_scalar(
                    out=cols[:, 17:18], in0=cols[:, 32 + i:33 + i],
                    scalar1=wcol[:, 1:2], scalar2=None, op0=ALU.mult))(i),
                    r=[f"sxinv{i}", "wcol"], w=["epiu"])
                A("scalar", lambda s_: s_.activation(
                    out=gate[:], in_=ps_g[:], func=AF.Silu, scale=cols[:, 16:17]),
                    r=["ps_g", "epig"], w=["gate"])
                A("scalar", lambda s_: s_.activation(
                    out=up[:], in_=ps_u[:], func=AF.Copy, scale=cols[:, 17:18]),
                    r=["ps_u", "epiu"], w=["up"])
                A("vector", lambda v: v.tensor_mul(
                    out=hbuf[:], in0=gate[:], in1=up[:]), r=["gate", "up"], w=["hbuf"])
                A("sync", (lambda i, n: lambda s_: s_.dma_start(
                    out=h_t[i][:, n * 512:(n + 1) * 512], in_=hbuf[:]))(i, n),
                    r=["hbuf"], w=[f"h{i}_{n}"], dma=True)

        wd_t = wdT_d.rearrange("(n p) d -> n p d", p=128)
        for c in range(32):
            b, bk = [(wstg, "wstg"), (wstg2, "wstg2")][c % 2]
            A("sync", (lambda c, b: lambda s_: s_.dma_start(
                out=b[:, 0:1024], in_=wd_t[c]))(c, b), w=[bk], dma=True)
            emit_wquant_chunk(sch, wqd[:, c * 1024:(c + 1) * 1024], "wqd",
                              b[:, 0:1024], bk, wcol[:, 6:7])
        for i in range(NT):
            A("sync", (lambda i: lambda s_: s_.dma_start(
                out=hrow[:], in_=h_t[i]))(i),
                r=[f"h{i}_{n}" for n in range(8)], w=["hrow"], dma=True)
            A("sync", (lambda i: lambda s_: s_.dma_start(
                out=xtile[:], in_=x_t[i]))(i), w=["xtile"], dma=True)
            emit_actquant(sch, hrow[:], "hrow", hq[:], "hq", cols, 20, 127,
                          inv_dst_ap=cols[:, 24:25], inv_dst_buf="shinv")
            for c in range(32):
                emit_transpose(sch, hq[:, c * 128:(c + 1) * 128], "hq",
                               hqT[:, c * 128:(c + 1) * 128], "hqT", ps_t, ident)
            A("vector", lambda v: v.tensor_scalar(
                out=cols[:, 25:26], in0=cols[:, 24:25], scalar1=wcol[:, 2:3],
                scalar2=0.5, op0=ALU.mult, op1=ALU.mult),
                r=["shinv", "wcol"], w=["epid"])
            for n in range(2):
                for c in range(32):
                    lhsT = hqT[:, c * 128:(c + 1) * 128]
                    A("tensor", (lambda c, n, lhsT: lambda t: t.matmul(
                        out=ps_d[:], lhsT=lhsT,
                        rhs=wqd[:, c * 1024 + n * 512:c * 1024 + (n + 1) * 512],
                        start=(c == 0), stop=(c == 31)))(c, n, lhsT),
                        r=["hqT", "wqd"], w=["ps_d"])
                A("vector", (lambda i, n: lambda v: v.scalar_tensor_tensor(
                    out=outt[:], in0=ps_d[:], scalar=cols[:, 25:26],
                    in1=xtile[:, n * 512:(n + 1) * 512],
                    op0=ALU.mult, op1=ALU.add))(i, n),
                    r=["ps_d", "epid", "xtile"], w=["outt"])
                A("sync", (lambda i, n: lambda s_: s_.dma_start(
                    out=xo_d.rearrange("(m p) d -> m p d", p=128)[i][:, n * 512:(n + 1) * 512],
                    in_=outt[:]))(i, n), r=["outt"], w=[f"xo{i}_{n}"], dma=True)

        if debug:
            A("sync", lambda s_: s_.dma_start(out=dbg_d[:, 0:64], in_=cols[:]),
              r=["epid"], dma=True)
            A("sync", lambda s_: s_.dma_start(out=dbg_d[:, 64:72], in_=wcol[:]),
              r=["wcol"], dma=True)
            A("sync", lambda s_: s_.dma_start(out=dbg_d[:, 72:72+1024], in_=ln[:]),
              r=["ln"], dma=True)
            A("sync", lambda s_: s_.dma_start(out=dbg_d[:, 1096:1096+1024], in_=hrow[:, 0:1024]),
              r=["hrow"], dma=True)
        make_block(nc, ctx, sch, final_eng="gpsimd")
    return nc



def host_consts():
    ident = np.eye(128, dtype=np.float32)
    onescol = np.ones((128, 1), np.float32)
    onesrow = np.ones((1, 128), np.float32)
    return {
        "ident": _bf16(ident),
        "onescol": _f32(onescol),
        "onesrow": _f32(onesrow),
    }


_FFN0_CACHE = {}


def run_ffn0(x_full, wg, wu, wd):
    import os
    dbg = bool(int(os.environ.get("FFN0_DEBUG", "0")))
    if "nc" not in _FFN0_CACHE:
        _FFN0_CACHE["nc"] = build_ffn0(debug=dbg)
    nc = _FFN0_CACHE["nc"]
    consts = host_consts()
    wgT = _f32(wg.T)
    wuT = _f32(wu.T)
    wdT = _f32(wd.T)
    shards = np.split(_f32(x_full), NCORE, axis=0)
    in_maps = []
    for c in range(NCORE):
        m = {"x": shards[c], "wgT": wgT, "wuT": wuT, "wdT": wdT}
        m.update(consts)
        in_maps.append(m)
    res = run_bass_kernel_spmd(nc, in_maps, core_ids=list(range(NCORE)))
    if dbg:
        np.save("/tmp/ffn0_dbg.npy", res.results[0]["dbg"])
    return np.concatenate([r["xo"] for r in res.results], axis=0)





H = 16
DK = 64
NBLK = 512
NPAIR = 8


def build_attn():
    nc = bass.Bass("TRN2", target_bir_lowering=False, debug=False)
    x_d = nc.dram_tensor("x", [T, D], F32, kind="ExternalInput").ap()
    wqkvT_d = nc.dram_tensor("wqkvT", [D, 3 * D], F32, kind="ExternalInput").ap()
    woT_d = nc.dram_tensor("woT", [D, D], F32, kind="ExternalInput").ap()
    identf_d = nc.dram_tensor("identf", [128, 128], F32, kind="ExternalInput").ap()
    identb_d = nc.dram_tensor("identb", [128, 128], BF16, kind="ExternalInput").ap()
    pc_names = ["Vgs", "VT0", "btau", "bgm", "Cch", "gam", "Wltm"]
    pcs_d = {n: nc.dram_tensor("pc_" + n, [128, NBLK], F32, kind="ExternalInput").ap()
             for n in pc_names}
    alpha_d = nc.dram_tensor("alphaIC", [128, 2], F32, kind="ExternalInput").ap()
    seldp_d = nc.dram_tensor("seldp", [64, 32 * 128], BF16, kind="ExternalInput").ap()
    pat_d = nc.dram_tensor("pat", [128, 64 * 128], BF16, kind="ExternalInput").ap()
    iota1_d = nc.dram_tensor("iota1", [128, T], F32, kind="ExternalInput").ap()
    steps_d = nc.dram_tensor("steps", [128, 8], F32, kind="ExternalInput").ap()
    gates_d = nc.dram_tensor("gates", [128, 8], F32, kind="ExternalInput").ap()
    xo_d = nc.dram_tensor("xo", [T, D], F32, kind="ExternalOutput").ap()
    qkv_d = nc.dram_tensor("qkv_scratch", [T, 3 * D], F32).ap()
    yt_d = nc.dram_tensor("yt_scratch", [NPAIR * 128, T], BF16).ap()
    ccin_d = [nc.dram_tensor(f"ccin{hp}", [64, 128], F32) for hp in range(NPAIR)]
    ccout_d = [nc.dram_tensor(f"ccout{hp}", [8 * 64, 128], F32) for hp in range(NPAIR)]

    with ExitStack() as ctx:
        e = ctx.enter_context
        identf = e(nc.sbuf_tensor("identf_s", [128, 128], F32))
        identb = e(nc.sbuf_tensor("identb_s", [128, 128], BF16))
        pcs = {n: e(nc.sbuf_tensor("pcs_" + n, [128, NBLK], F32)) for n in
               ["btau", "Cch", "Wltm"]}
        Rcol = e(nc.sbuf_tensor("Rcol", [128, NBLK], F32))
        logR = e(nc.sbuf_tensor("logR", [128, NBLK], F32))
        cfcol = e(nc.sbuf_tensor("cfcol", [128, NBLK], F32))
        ptmp = e(nc.sbuf_tensor("ptmp", [128, NBLK], F32))
        ptmp2 = e(nc.sbuf_tensor("ptmp2", [128, NBLK], F32))
        alphaIC = e(nc.sbuf_tensor("alphaIC_s", [128, 2], F32))
        seldp = e(nc.sbuf_tensor("seldp_s", [64, 32 * 128], BF16))
        pat = e(nc.sbuf_tensor("pat_s", [128, 64 * 128], BF16))
        iota1 = e(nc.sbuf_tensor("iota1_s", [128, T], F32))
        steps = e(nc.sbuf_tensor("steps_s", [128, 8], F32))
        gates = e(nc.sbuf_tensor("gates_s", [128, 8], F32))
        onecol = e(nc.sbuf_tensor("onecol", [128, 1], F32))
        epscol = e(nc.sbuf_tensor("epscol", [128, 1], F32))
        xtile = e(nc.sbuf_tensor("xtile", [128, D], F32))
        lnsc = e(nc.sbuf_tensor("lnsc", [128, D], F32))
        lnb = e(nc.sbuf_tensor("lnb", [128, D], BF16))
        lnbT = e(nc.sbuf_tensor("lnbT", [128, 8 * 128], BF16))
        qkvt = e(nc.sbuf_tensor("qkvt", [128, 3 * D], F32))
        wqkv_all = e(nc.sbuf_tensor("wqkv_all", [128, 8 * 3 * D], BF16))
        kcols = e(nc.sbuf_tensor("kcols", [128, 48], F32))
        cols = e(nc.sbuf_tensor("cols", [128, 16], F32))
        qT2 = e(nc.sbuf_tensor("qT2", [128, 2 * T], F32))
        kT2 = e(nc.sbuf_tensor("kT2", [128, 2 * T], F32))
        vT = e(nc.sbuf_tensor("vT", [64, 2 * T], BF16))
        tpinb = e(nc.sbuf_tensor("tpinb", [128, 64], BF16))
        tpin = e(nc.sbuf_tensor("tpin", [128, 64], F32))
        ubuf = e(nc.sbuf_tensor("ubuf", [128, T], F32))
        Fbuf = e(nc.sbuf_tensor("Fbuf", [128, T], F32))
        Ptile = e(nc.sbuf_tensor("Ptile", [128, T], F32))
        qmb = e(nc.sbuf_tensor("qmb", [128, T], BF16))
        sig = e(nc.sbuf_tensor("sig", [128, 64], F32))
        agbuf = e(nc.sbuf_tensor("agbuf", [128, 64], F32))
        f0 = e(nc.sbuf_tensor("f0", [128, 64], F32))
        ytb = e(nc.sbuf_tensor("ytb", [128, T], BF16))
        ypc = e(nc.sbuf_tensor("ypc", [128, 128], BF16))
        outt = e(nc.sbuf_tensor("outt", [128, 512], F32))
        ps_v = e(nc.psum_tensor("ps_v", [128, 512], F32))
        ps_y = [e(nc.psum_tensor(f"ps_y{c}", [128, 512], F32)) for c in range(4)]
        ps_t = e(nc.psum_tensor("ps_tf", [128, 512], F32))
        ps_tb = e(nc.psum_tensor("ps_tbb", [128, 1024], BF16))
        ps_o = e(nc.psum_tensor("ps_o", [128, 512], F32))

        sch = Sched(nc, ctx)
        A = sch.add

        A("sync", lambda s_: s_.dma_start(out=identf[:], in_=identf_d), w=["identf_s"], dma=True)
        A("sync", lambda s_: s_.dma_start(out=identb[:], in_=identb_d), w=["identb_s"], dma=True)
        for n in ["btau", "Cch", "Wltm"]:
            A("sync", (lambda n: lambda s_: s_.dma_start(out=pcs[n][:], in_=pcs_d[n]))(n),
              w=["pcs_" + n], dma=True)
        A("sync", lambda s_: s_.dma_start(out=alphaIC[:], in_=alpha_d), w=["alphaIC"], dma=True)
        A("sync", lambda s_: s_.dma_start(out=seldp[:], in_=seldp_d), w=["seldp"], dma=True)
        A("sync", lambda s_: s_.dma_start(out=pat[:], in_=pat_d), w=["pat"], dma=True)
        A("sync", lambda s_: s_.dma_start(out=iota1[:], in_=iota1_d), w=["iota1"], dma=True)
        A("sync", lambda s_: s_.dma_start(out=steps[:], in_=steps_d), w=["steps"], dma=True)
        A("sync", lambda s_: s_.dma_start(out=gates[:], in_=gates_d), w=["gates"], dma=True)
        A("vector", lambda v: v.memset(epscol[:], LN_EPS), w=["epscol"])
        A("vector", lambda v: v.memset(onecol[:], 1.0), w=["onecol"])

        A("sync", lambda s_: s_.dma_start(out=Rcol[:], in_=pcs_d["Vgs"]), w=["Rcol"], dma=True)
        A("sync", lambda s_: s_.dma_start(out=logR[:], in_=pcs_d["VT0"]), w=["logR"], dma=True)
        A("vector", lambda v: v.tensor_sub(out=ptmp[:], in0=Rcol[:], in1=logR[:]),
          r=["Rcol", "logR"], w=["ptmp"])
        A("vector", lambda v: v.tensor_add(out=ptmp[:], in0=ptmp[:], in1=pcs["Wltm"][:]),
          r=["ptmp", "pcs_Wltm"], w=["ptmp"])
        A("scalar", lambda s_: s_.activation(out=ptmp2[:], in_=ptmp[:], func=AF.Abs),
          r=["ptmp"], w=["ptmp2"])
        A("scalar", lambda s_: s_.activation(out=ptmp2[:], in_=ptmp2[:], func=AF.Exp,
          scale=-1.0), r=["ptmp2"], w=["ptmp2"])
        A("scalar", lambda s_: s_.activation(out=ptmp2[:], in_=ptmp2[:], func=AF.Ln,
          bias=onecol[:, 0:1]), r=["ptmp2", "onecol"], w=["ptmp2"])
        A("vector", lambda v: v.tensor_relu(out=cfcol[:], in_=ptmp[:]),
          r=["ptmp"], w=["cfcol"])
        A("vector", lambda v: v.tensor_add(out=ptmp2[:], in0=ptmp2[:], in1=cfcol[:]),
          r=["ptmp2", "cfcol"], w=["ptmp2"])
        A("vector", lambda v: v.tensor_mul(out=Rcol[:], in0=ptmp2[:], in1=pcs["btau"][:]),
          r=["ptmp2", "pcs_btau"], w=["Rcol"])
        A("vector", lambda v: v.reciprocal(out=logR[:], in_=pcs["Cch"][:]),
          r=["pcs_Cch"], w=["logR"])
        A("vector", lambda v: v.scalar_tensor_tensor(out=logR[:], in0=Rcol[:],
          scalar=-1.0, in1=logR[:], op0=ALU.mult, op1=ALU.mult),
          r=["Rcol", "logR"], w=["logR"])
        A("vector", lambda v: v.tensor_scalar(out=Fbuf[:, 0:NBLK], in0=Rcol[:],
          scalar1=alphaIC[:, 1:2], scalar2=alphaIC[:, 0:1], op0=ALU.subtract,
          op1=ALU.mult), r=["Rcol", "alphaIC"], w=["Fbuf"])
        A("scalar", lambda s_: s_.activation(out=Fbuf[:, 0:NBLK], in_=Fbuf[:, 0:NBLK],
          func=AF.Tanh), r=["Fbuf"], w=["Fbuf"])
        A("scalar", lambda s_: s_.activation(out=ubuf[:, 0:NBLK], in_=ptmp[:],
          func=AF.Sigmoid), r=["ptmp"], w=["ubuf"])
        A("vector", lambda v: v.tensor_mul(out=ubuf[:, 0:NBLK], in0=ubuf[:, 0:NBLK],
          in1=ptmp2[:]), r=["ubuf", "ptmp2"], w=["ubuf"])
        A("sync", lambda s_: s_.dma_start(out=ptmp2[:], in_=pcs_d["bgm"]),
          w=["ptmp2"], dma=True)
        A("vector", lambda v: v.tensor_mul(out=ubuf[:, 0:NBLK], in0=ubuf[:, 0:NBLK],
          in1=ptmp2[:]), r=["ubuf", "ptmp2"], w=["ubuf"])
        A("sync", lambda s_: s_.dma_start(out=ptmp2[:], in_=pcs_d["gam"]),
          w=["ptmp2"], dma=True)
        A("vector", lambda v: v.tensor_mul(out=cfcol[:], in0=Fbuf[:, 0:NBLK],
          in1=ubuf[:, 0:NBLK]), r=["Fbuf", "ubuf"], w=["cfcol"])
        A("vector", lambda v: v.tensor_mul(out=cfcol[:], in0=cfcol[:], in1=ptmp2[:]),
          r=["cfcol", "ptmp2"], w=["cfcol"])
        A("scalar", lambda s_: s_.activation(out=Rcol[:], in_=logR[:], func=AF.Exp),
          r=["logR"], w=["Rcol"])

        for k in range(8):
            A("sync", (lambda k: lambda s_: s_.dma_start(
                out=qkvt[:], in_=wqkvT_d.rearrange("(k p) f -> k p f", p=128)[k]))(k),
                w=["qkvt"], dma=True)
            A("vector", (lambda k: lambda v: v.tensor_copy(
                out=wqkv_all[:, k * 3 * D:(k + 1) * 3 * D], in_=qkvt[:]))(k),
                r=["qkvt"], w=["wqkv_all"])

        x_t = x_d.rearrange("(n p) d -> n p d", p=128)
        qkv_t = qkv_d.rearrange("(n p) d -> n p d", p=128)
        for i in range(NT):
            A("sync", (lambda i: lambda s_: s_.dma_start(
                out=xtile[:], in_=x_t[i]))(i), w=["xtile"], dma=True)
            emit_ln(sch, xtile[:], "xtile", lnb[:], "lnb", lnsc[:], cols, epscol[:, 0:1])
            for c in range(8):
                emit_transpose(sch, lnb[:, c * 128:(c + 1) * 128], "lnb",
                               lnbT[:, c * 128:(c + 1) * 128], "lnbT", ps_tb, identb,
                               ps_key="ps_tbb", ident_key="identb_s")
            for n in range(6):
                for c in range(8):
                    A("tensor", (lambda n, c: lambda t: t.matmul(
                        out=ps_o[:], lhsT=lnbT[:, c * 128:(c + 1) * 128],
                        rhs=wqkv_all[:, c * 3 * D + n * 512:c * 3 * D + (n + 1) * 512],
                        start=(c == 0), stop=(c == 7)))(n, c),
                        r=["lnbT", "wqkv_all"], w=["ps_o"])
                A("vector", (lambda n: lambda v: v.tensor_copy(
                    out=qkvt[:, n * 512:(n + 1) * 512], in_=ps_o[:]))(n),
                    r=["ps_o"], w=["qkvt"])
            kv = qkvt[:, D:2 * D]
            kv3 = kv.rearrange("p (h e) -> p h e", e=64)
            A("vector", lambda v: v.reduce_sum(out=kcols[:, 0:16], in_=kv3,
              axis=AX.X), r=["qkvt"], w=["kc1"])
            A("vector", lambda v: v.tensor_scalar(out=kcols[:, 0:16],
              in0=kcols[:, 0:16], scalar1=1.0 / 64, scalar2=None, op0=ALU.mult),
              r=["kc1"], w=["kc1"])
            A("scalar", lambda s_: s_.activation(out=lnsc[:], in_=kv, func=AF.Square),
              r=["qkvt"], w=["lnsc"])
            A("vector", lambda v: v.reduce_sum(out=kcols[:, 16:32],
              in_=lnsc[:].rearrange("p (h e) -> p h e", e=64), axis=AX.X),
              r=["lnsc"], w=["kc2"])
            A("vector", lambda v: v.tensor_scalar(out=kcols[:, 16:32],
              in0=kcols[:, 16:32], scalar1=1.0 / 64, scalar2=None, op0=ALU.mult),
              r=["kc2"], w=["kc2"])
            A("vector", lambda v: v.tensor_mul(out=kcols[:, 32:48],
              in0=kcols[:, 0:16], in1=kcols[:, 0:16]), r=["kc1"], w=["kc3"])
            A("vector", lambda v: v.tensor_sub(out=kcols[:, 16:32],
              in0=kcols[:, 16:32], in1=kcols[:, 32:48]), r=["kc2", "kc3"], w=["kc2"])
            A("scalar", lambda s_: s_.activation(out=kcols[:, 16:32],
              in_=kcols[:, 16:32], func=AF.Ln, bias=epscol[:, 0:1]),
              r=["kc2", "epscol"], w=["kc2"])
            A("scalar", lambda s_: s_.activation(out=kcols[:, 16:32],
              in_=kcols[:, 16:32], func=AF.Exp, scale=-0.5), r=["kc2"], w=["kc2"])
            A("vector", lambda v: v.tensor_tensor(out=kv3, in0=kv3,
              in1=kcols[:, 0:16].unsqueeze(2).to_broadcast([128, 16, 64]),
              op=ALU.subtract), r=["qkvt", "kc1"], w=["qkvt"])
            A("vector", lambda v: v.tensor_tensor(out=kv3, in0=kv3,
              in1=kcols[:, 16:32].unsqueeze(2).to_broadcast([128, 16, 64]),
              op=ALU.mult), r=["qkvt", "kc2"], w=["qkvt"])
            A("sync", (lambda i: lambda s_: s_.dma_start(
                out=qkv_t[i], in_=qkvt[:]))(i), r=["qkvt"], w=[f"qkvd{i}"], dma=True)

        yt_t = yt_d.rearrange("(hp p) t -> hp p t", p=128)
        for hp in range(NPAIR):
            for hh, h in ((0, 2 * hp), (1, 2 * hp + 1)):
                for (buf, bufk, coloff, dbl) in ((qT2, "qT2", 0, True),
                                                 (kT2, "kT2", D, True),
                                                 (vT, "vT", 2 * D, False)):
                  for i in range(NT):
                        A("sync", (lambda i, coloff, h: lambda s_: s_.dma_start(
                            out=tpin[:], in_=qkv_t[i][:, coloff + h * 64:coloff + (h + 1) * 64]))(i, coloff, h),
                            deps=[sch.last_w.get(f"qkvd{i}")], w=["tpin"], dma=True)
                        if not dbl:
                            A("vector", lambda v: v.tensor_copy(out=tpinb[:, 0:64],
                              in_=tpin[:, 0:64]), r=["tpin"], w=["tpinb"])
                            A("tensor", lambda t: t.transpose(
                                out=ps_tb[0:64, 0:128], in_=tpinb[:, 0:64], identity=identb[:]),
                                r=["tpinb", "identb_s"], w=["ps_tbb"])
                            A("vector", (lambda buf, hh, i: lambda v: v.tensor_copy(
                                out=buf[0:64, hh * T + i * 128:hh * T + (i + 1) * 128],
                                in_=ps_tb[0:64, 0:128]))(buf, hh, i),
                                r=["ps_tbb"], w=[bufk])
                            continue
                        A("tensor", lambda t: t.transpose(
                            out=ps_t[0:64, 0:128], in_=tpin[:, 0:64],
                            identity=identf[:]),
                            r=["tpin", "identf_s"], w=["ps_tf"])
                        if dbl:
                            A("vector", (lambda buf, hh, i: lambda v: v.tensor_copy(
                                out=buf[0:64, hh * T + i * 128:hh * T + (i + 1) * 128],
                                in_=ps_t[0:64, 0:128]))(buf, hh, i),
                                r=["ps_tf"], w=[bufk])

                  if dbl:
                      A("gpsimd", (lambda buf, hh: lambda g: g.dma_start(
                          out=buf[64:128, hh * T:(hh + 1) * T],
                          in_=buf[0:64, hh * T:(hh + 1) * T]))(buf, hh),
                          r=[bufk], w=[bufk], dma=True)

            for j in range(64):
                hh, dp = j >> 5, j & 31
                jj = hp * 64 + j
                for c in range(4):
                    pv, pvk = ps_v, "ps_v"
                    A("tensor", (lambda c, hh, dp, pv: lambda t: t.matmul(
                        out=pv[:], lhsT=seldp[:, dp * 128:(dp + 1) * 128],
                        rhs=vT[0:64, hh * T + c * 512:hh * T + (c + 1) * 512],
                        start=True, stop=True))(c, hh, dp, pv),
                        r=["seldp", "vT"], w=[pvk])
                    A("vector", (lambda c, hh, jj, pv, pvk: lambda v: v.scalar_tensor_tensor(
                        out=ubuf[:, c * 512:(c + 1) * 512], in0=pv[:],
                        scalar=cfcol[:, jj:jj + 1],
                        in1=kT2[:, hh * T + c * 512:hh * T + (c + 1) * 512],
                        op0=ALU.mult, op1=ALU.mult))(c, hh, jj, pv, pvk),
                        r=[pvk, "cfcol", "kT2"], w=["ubuf"])
                A("vector", (lambda jj: lambda v: v.tensor_tensor_scan(
                    out=Fbuf[:], data0=Rcol[:, jj:jj + 1].to_broadcast([128, T]),
                    data1=ubuf[:], initial=0.0, op0=ALU.mult, op1=ALU.add))(jj),
                    r=["Rcol", "ubuf"], w=["Fbuf"])
                A("vector", (lambda j: lambda v: v.tensor_copy(
                    out=sig[:, j:j + 1], in_=Fbuf[:, T - 1:T]))(j),
                    r=["Fbuf"], w=["sig"])
                A("vector", (lambda jj, hh: lambda v: v.scalar_tensor_tensor(
                    out=qmb[:], in0=Fbuf[:], scalar=pcs["Wltm"][:, jj:jj + 1],
                    in1=qT2[:, hh * T:(hh + 1) * T],
                    op0=ALU.add, op1=ALU.mult))(jj, hh),
                    r=["Fbuf", "pcs_Wltm", "qT2"], w=["qmb"])
                for c in range(4):
                    A("tensor", (lambda j, c, hh, dp: lambda t: t.matmul(
                        out=ps_y[c][:], lhsT=pat[:, j * 128:(j + 1) * 128],
                        rhs=qmb[:, c * 512:(c + 1) * 512],
                        start=(j == 0), stop=False))(j, c, hh, dp),
                        r=["pat", "qmb"], w=[f"ps_y{c}"])

            A("gpsimd", (lambda hp: lambda g: g.dma_start(
                out=ccin_d[hp].ap(), in_=sig[:]))(hp), r=["sig"], w=["ccin"], dma=True)
            A("gpsimd", (lambda hp: lambda g: g.collective_compute(
                "AllGather", ALU.bypass, replica_groups=[list(range(NCORE))],
                ins=[ccin_d[hp].ap().opt()], outs=[ccout_d[hp].ap().opt()]))(hp),
                r=["ccin"], w=["ccout"])
            A("vector", lambda v: v.memset(f0[:], 0.0), w=["f0"])
            for m in range(8):
                A("gpsimd", (lambda hp, m: lambda g: g.dma_start(
                    out=agbuf[:], in_=ccout_d[hp].ap().rearrange(
                        "(m a) b -> m a b", a=64)[m].rearrange("a b -> (a b)").rearrange(
                        "(p f) -> p f", p=128)))(hp, m),
                    r=["ccout"], w=["agbuf"], dma=True)
                A("vector", (lambda hp, m: lambda v: v.tensor_scalar(
                    out=Ptile[:, 0:64], in0=logR[:, hp * 64:hp * 64 + 64],
                    scalar1=steps[:, m:m + 1], scalar2=None, op0=ALU.mult))(hp, m),
                    r=["logR", "steps"], w=["Ptile"])
                A("scalar", lambda s_: s_.activation(out=Ptile[:, 0:64],
                  in_=Ptile[:, 0:64], func=AF.Exp), r=["Ptile"], w=["Ptile"])
                A("vector", (lambda m: lambda v: v.tensor_scalar(
                    out=Ptile[:, 0:64], in0=Ptile[:, 0:64],
                    scalar1=gates[:, m:m + 1], scalar2=None, op0=ALU.mult))(m),
                    r=["Ptile", "gates"], w=["Ptile"])
                A("vector", lambda v: v.tensor_mul(out=Ptile[:, 0:64],
                  in0=Ptile[:, 0:64], in1=agbuf[:]), r=["Ptile", "agbuf"], w=["Ptile"])
                A("vector", lambda v: v.tensor_add(out=f0[:], in0=f0[:],
                  in1=Ptile[:, 0:64]), r=["Ptile", "f0"], w=["f0"])

            for j in range(64):
                hh, dp = j >> 5, j & 31
                jj = hp * 64 + j
                A("scalar", (lambda jj: lambda s_: s_.activation(
                    out=Ptile[:], in_=iota1[:], func=AF.Exp,
                    scale=logR[:, jj:jj + 1]))(jj),
                    r=["iota1", "logR"], w=["Ptile"])
                A("vector", (lambda j, hh: lambda v: v.scalar_tensor_tensor(
                    out=qmb[:], in0=Ptile[:], scalar=f0[:, j:j + 1],
                    in1=qT2[:, hh * T:(hh + 1) * T],
                    op0=ALU.mult, op1=ALU.mult))(j, hh),
                    r=["Ptile", "f0", "qT2"], w=["qmb"])
                for c in range(4):
                    A("tensor", (lambda j, c: lambda t: t.matmul(
                        out=ps_y[c][:], lhsT=pat[:, j * 128:(j + 1) * 128],
                        rhs=qmb[:, c * 512:(c + 1) * 512],
                        start=False, stop=(j == 63)))(j, c),
                        r=["pat", "qmb"], w=[f"ps_y{c}"])

            for c in range(4):
                A("vector", (lambda c: lambda v: v.tensor_copy(
                    out=ytb[:, c * 512:(c + 1) * 512], in_=ps_y[c][:]))(c),
                    r=[f"ps_y{c}"], w=["ytb"])
            A("sync", (lambda hp: lambda s_: s_.dma_start(
                out=yt_t[hp], in_=ytb[:]))(hp), r=["ytb"], w=[f"ytd{hp}"], dma=True)

        wob = wqkv_all
        for k in range(8):
            A("sync", (lambda k: lambda s_: s_.dma_start(
                out=qkvt[:, 0:D], in_=woT_d.rearrange("(k p) f -> k p f", p=128)[k]))(k),
                w=["qkvt"], dma=True)
            A("vector", (lambda k: lambda v: v.tensor_copy(
                out=wob[:, k * D:(k + 1) * D], in_=qkvt[:, 0:D]))(k),
                r=["qkvt"], w=["wqkv_all"])
        for i in range(NT):
            A("sync", (lambda i: lambda s_: s_.dma_start(
                out=xtile[:], in_=x_t[i]))(i), w=["xtile"], dma=True)
            for n in range(2):
                for hp in range(NPAIR):
                    A("sync", (lambda i, hp: lambda s_: s_.dma_start(
                        out=ypc[:], in_=yt_t[hp][:, i * 128:(i + 1) * 128]))(i, hp),
                        deps=[sch.last_w.get(f"ytd{hp}")], w=["ypc"], dma=True)
                    A("tensor", (lambda i, n, hp: lambda t: t.matmul(
                        out=ps_o[:], lhsT=ypc[:],
                        rhs=wob[:, hp * D + n * 512:hp * D + (n + 1) * 512],
                        start=(hp == 0), stop=(hp == 7)))(i, n, hp),
                        r=["ypc", "wqkv_all"], w=["ps_o"])
                A("vector", (lambda i, n: lambda v: v.tensor_add(
                    out=outt[:], in0=ps_o[:],
                    in1=xtile[:, n * 512:(n + 1) * 512]))(i, n),
                    r=["ps_o", "xtile"], w=["outt"])
                A("sync", (lambda i, n: lambda s_: s_.dma_start(
                    out=xo_d.rearrange("(m p) d -> m p d", p=128)[i][:, n * 512:(n + 1) * 512],
                    in_=outt[:]))(i, n), r=["outt"], w=[f"xo{i}_{n}"], dma=True)

        make_block(nc, ctx, sch, final_eng="gpsimd")
    return nc


def attn_host_layout(W_LTM, V_gs, V_T0, beta_tau, beta_gm, C_ch, gamma,
                     alpha_ppd, IC_threshold, core_id):
    hidx = np.arange(NBLK) // 32
    dpidx = np.arange(NBLK) % 32
    p = np.arange(128)
    dd = p >> 6
    ee = p & 63
    out = {}
    Wl = W_LTM[hidx[None, :], (2 * dpidx[None, :] + dd[:, None]), ee[:, None]]
    out["pc_Wltm"] = _f32(Wl)
    for name, vec in (("Vgs", V_gs), ("VT0", V_T0), ("btau", beta_tau),
                      ("bgm", beta_gm), ("Cch", C_ch), ("gam", gamma)):
        out["pc_" + name] = _f32(np.broadcast_to(vec[hidx][None, :], (128, NBLK)))
    out["alphaIC"] = _f32(np.broadcast_to(
        np.array([alpha_ppd.reshape(()), IC_threshold.reshape(())])[None, :], (128, 2)))
    d_in = np.arange(64)
    sel = (d_in[:, None, None] == (2 * np.arange(32)[None, :, None] + dd[None, None, :]))
    out["seldp"] = _bf16(sel.reshape(64, 32 * 128))
    jarr = np.arange(64)
    m = np.arange(128)
    tgt = (jarr[None, :] >> 5) * 64 + 2 * (jarr[None, :] & 31) + dd[:, None]
    patt = (m[None, None, :] == tgt[:, :, None])
    out["pat"] = _bf16(patt.reshape(128, 64 * 128))
    out["iota1"] = _f32(np.broadcast_to(np.arange(1, T + 1, dtype=np.float64)[None, :], (128, T)))
    steps = np.zeros(8, np.float64)
    gts = np.zeros(8, np.float64)
    for mm in range(8):
        if mm < core_id:
            steps[mm] = float(T) * (core_id - 1 - mm)
            gts[mm] = 1.0
    out["steps"] = _f32(np.broadcast_to(steps[None, :], (128, 8)))
    out["gates"] = _f32(np.broadcast_to(gts[None, :], (128, 8)))
    return out


_ATTN_CACHE = {}


def run_attn(x_full, W_qkv, W_o, W_LTM, V_gs, V_T0, beta_tau, beta_gm, C_ch,
             gamma, alpha_ppd, IC_threshold):
    if "nc" not in _ATTN_CACHE:
        _ATTN_CACHE["nc"] = build_attn()
    nc = _ATTN_CACHE["nc"]
    wqkvT = _f32(np.asarray(W_qkv).T)
    woT = _f32(np.asarray(W_o).T)
    identf = _f32(np.eye(128))
    shards = np.split(_f32(x_full), NCORE, axis=0)
    in_maps = []
    for c in range(NCORE):
        m = {"x": shards[c], "wqkvT": wqkvT, "woT": woT, "identf": identf,
             "identb": _bf16(np.eye(128))}
        m.update(attn_host_layout(np.asarray(W_LTM), np.asarray(V_gs),
                                  np.asarray(V_T0), np.asarray(beta_tau),
                                  np.asarray(beta_gm), np.asarray(C_ch),
                                  np.asarray(gamma), np.asarray(alpha_ppd),
                                  np.asarray(IC_threshold), c))
        in_maps.append(m)
    res = run_bass_kernel_spmd(nc, in_maps, core_ids=list(range(NCORE)))
    return np.concatenate([r["xo"] for r in res.results], axis=0)





DI = 2048
NDB = 16
NST = 16


def build_mamba(debug=False):
    nc = bass.Bass("TRN2", target_bir_lowering=False, debug=False)
    x_d = nc.dram_tensor("x", [T, D], F32, kind="ExternalInput").ap()
    win_d = nc.dram_tensor("win", [D, 2 * DI], F32, kind="ExternalInput").ap()
    wxp_d = nc.dram_tensor("wxp", [DI, 33], F32, kind="ExternalInput").ap()
    wout_d = nc.dram_tensor("wout", [DI, D], F32, kind="ExternalInput").ap()
    convw_d = nc.dram_tensor("convw", [128, 4 * NDB], F32, kind="ExternalInput").ap()
    convb_d = nc.dram_tensor("convb", [128, NDB], F32, kind="ExternalInput").ap()
    wdt_d = nc.dram_tensor("wdt", [128, NDB], F32, kind="ExternalInput").ap()
    bdt_d = nc.dram_tensor("bdt", [128, NDB], F32, kind="ExternalInput").ap()
    alog_d = nc.dram_tensor("alog", [128, NDB * NST], F32, kind="ExternalInput").ap()
    dpar_d = nc.dram_tensor("dpar", [128, NDB], F32, kind="ExternalInput").ap()
    identf_d = nc.dram_tensor("identf", [128, 128], F32, kind="ExternalInput").ap()
    identb_d = nc.dram_tensor("identb", [128, 128], BF16, kind="ExternalInput").ap()
    onesrow_d = nc.dram_tensor("onesrow", [1, 128], F32, kind="ExternalInput").ap()
    hsel_d = nc.dram_tensor("hsel", [128, 8], F32, kind="ExternalInput").ap()
    selbc_d = nc.dram_tensor("selbc", [33, 33 * 128], F32, kind="ExternalInput").ap()
    capg_d = nc.dram_tensor("capg", [128, 8], F32, kind="ExternalInput").ap()
    xo_d = nc.dram_tensor("xo", [T, D], F32, kind="ExternalOutput").ap()
    xin_d = nc.dram_tensor("xin_scratch", [T, DI], F32).ap()
    z_d = nc.dram_tensor("z_scratch", [T, DI], F32).ap()
    xct_d = nc.dram_tensor("xct_scratch", [DI, T], F32).ap()
    y_d = nc.dram_tensor("y_scratch", [DI, T], F32, kind="ExternalOutput").ap()
    ygb_d = nc.dram_tensor("ygb_scratch", [DI, T], BF16).ap()
    xcopy_d = nc.dram_tensor("xcopy_scratch", [T, D], F32).ap()
    hl_in = nc.dram_tensor("hl_in", [48, 128], F32).ap()
    hl_out = nc.dram_tensor("hl_out", [8 * 48, 128], F32).ap()
    st_in = nc.dram_tensor("st_in", [512, 128], F32).ap()
    st_out = nc.dram_tensor("st_out", [8 * 512, 128], F32).ap()
    if debug:
        dbg_d = nc.dram_tensor("dbg", [128, 2048 * 6], F32, kind="ExternalOutput").ap()

    with ExitStack() as ctx:
        e = ctx.enter_context
        identf = e(nc.sbuf_tensor("identf_s", [128, 128], F32))
        identb = e(nc.sbuf_tensor("identb_s", [128, 128], BF16))
        onesrow = e(nc.sbuf_tensor("onesrow_s", [1, 128], F32))
        convw = e(nc.sbuf_tensor("convw_s", [128, 4 * NDB], F32))
        convb = e(nc.sbuf_tensor("convb_s", [128, NDB], F32))
        wdt = e(nc.sbuf_tensor("wdt_s", [128, NDB], F32))
        bdt = e(nc.sbuf_tensor("bdt_s", [128, NDB], F32))
        acol = e(nc.sbuf_tensor("acol_s", [128, NDB * NST], F32))
        dpar = e(nc.sbuf_tensor("dpar_s", [128, NDB], F32))
        hsel = e(nc.sbuf_tensor("hsel_s", [128, 8], F32))
        capg = e(nc.sbuf_tensor("capg_s", [128, 8], F32))
        selbc = e(nc.sbuf_tensor("selbc_s", [33, 33 * 128], F32))
        pfb = e(nc.sbuf_tensor("pfb", [128, 256], F32))
        sS = e(nc.sbuf_tensor("sS", [128, 256], F32))
        agb2 = e(nc.sbuf_tensor("agb2", [128, 256], F32))

        epscol = e(nc.sbuf_tensor("epscol", [128, 1], F32))
        onecol = e(nc.sbuf_tensor("onecol", [128, 1], F32))
        xtile = e(nc.sbuf_tensor("xtile", [128, D], F32))
        lnsc = e(nc.sbuf_tensor("lnsc", [128, D], F32))
        lnb = e(nc.sbuf_tensor("lnb", [128, D], BF16))
        lnbT = e(nc.sbuf_tensor("lnbT", [128, 8 * 128], BF16))
        cols = e(nc.sbuf_tensor("cols", [128, 16], F32))
        lnbTa = e(nc.sbuf_tensor("lnbTa", [128, NT * 8 * 128], BF16))
        wchk = e(nc.sbuf_tensor("wchk", [128, 8 * 512], BF16))
        wstg = e(nc.sbuf_tensor("wstg", [128, 2 * DI], F32))
        evt = e(nc.sbuf_tensor("evt", [128, 512], F32))
        xinb = e(nc.sbuf_tensor("xinb", [128, T + 3], F32))
        xcb = e(nc.sbuf_tensor("xcb", [128, T], F32))
        xcbb = e(nc.sbuf_tensor("xcbb", [128, T], BF16))
        wxpb = e(nc.sbuf_tensor("wxpb", [128, 33], BF16))
        xssm = e(nc.sbuf_tensor("xssm", [33, T], F32))
        brow = e(nc.sbuf_tensor("brow", [1, T], F32))
        crow = e(nc.sbuf_tensor("crow", [1, T], F32))
        delta = e(nc.sbuf_tensor("delta", [128, T], F32))
        dcum = e(nc.sbuf_tensor("dcum", [128, T], F32))
        wdx = e(nc.sbuf_tensor("wdx", [128, T], F32))
        dab = e(nc.sbuf_tensor("dab", [128, T], F32))
        ubuf = e(nc.sbuf_tensor("ubuf", [128, T], F32))
        hbuf = e(nc.sbuf_tensor("hbuf", [128, T], F32))
        ybuf = e(nc.sbuf_tensor("ybuf", [128, T], F32))
        sigm = e(nc.sbuf_tensor("sigm", [128, 256], F32))
        h0m = e(nc.sbuf_tensor("h0m", [128, 256], F32))
        agb = e(nc.sbuf_tensor("agb", [128, 256], F32))
        halo = e(nc.sbuf_tensor("halo", [128, 48], F32))
        outt = e(nc.sbuf_tensor("outt", [128, 512], F32))
        ps_m = e(nc.psum_tensor("ps_m", [128, 512], F32))
        ps_u = e(nc.psum_tensor("ps_u", [128, 2048], F32))
        ps_t = e(nc.psum_tensor("ps_tf", [128, 512], F32))
        ps_tb = e(nc.psum_tensor("ps_tbb", [128, 1024], BF16))

        halo3 = dcum
        halo3t = delta

        sch = Sched(nc, ctx, serialize_dma=False)
        A = sch.add

        for (t_s, t_d, kk) in ((identf, identf_d, "identf_s"), (identb, identb_d, "identb_s"),
                               (onesrow, onesrow_d, "onesrow_s"), (convw, convw_d, "convw_s"),
                               (convb, convb_d, "convb_s"), (wdt, wdt_d, "wdt_s"),
                               (bdt, bdt_d, "bdt_s"), (dpar, dpar_d, "dpar_s"),
                               (hsel, hsel_d, "hsel_s"), (capg, capg_d, "capg_s")):
            A("sync", (lambda t_s, t_d: lambda s_: s_.dma_start(out=t_s[:], in_=t_d))(t_s, t_d),
              w=[kk], dma=True)
        A("sync", lambda s_: s_.dma_start(out=acol[:], in_=alog_d), w=["acol_s"], dma=True)
        A("sync", lambda s_: s_.dma_start(out=selbc[:], in_=selbc_d), w=["selbc_s"], dma=True)
        A("vector", lambda v: v.memset(epscol[:], LN_EPS), w=["epscol"])
        A("vector", lambda v: v.memset(onecol[:], 1.0), w=["onecol"])
        A("scalar", lambda s_: s_.activation(out=acol[:], in_=acol[:], func=AF.Exp),
          r=["acol_s"], w=["acol_s"])
        A("vector", lambda v: v.tensor_scalar(out=acol[:], in0=acol[:], scalar1=-1.0,
          scalar2=None, op0=ALU.mult), r=["acol_s"], w=["acol_s"])

        x_t = x_d.rearrange("(n p) d -> n p d", p=128)
        xin_t = xin_d.rearrange("(n p) d -> n p d", p=128)
        z_t = z_d.rearrange("(n p) d -> n p d", p=128)
        xc_t = xcopy_d.rearrange("(n p) d -> n p d", p=128)
        for i in range(NT):
            A("sync", (lambda i: lambda s_: s_.dma_start(
                out=xtile[:], in_=x_t[i]))(i), w=["xtile"], dma=True)
            A("sync", (lambda i: lambda s_: s_.dma_start(
                out=xc_t[i], in_=xtile[:]))(i), r=["xtile"], w=[f"xcp{i}"], dma=True)
            emit_ln(sch, xtile[:], "xtile", lnb[:], "lnb", lnsc[:], cols, epscol[:, 0:1])
            for c in range(8):
                emit_transpose(sch, lnb[:, c * 128:(c + 1) * 128], "lnb",
                               lnbTa[:, (i * 8 + c) * 128:(i * 8 + c + 1) * 128],
                               "lnbTa", ps_tb, identb,
                               ps_key="ps_tbb", ident_key="identb_s")
        for n in range(8):
            A("sync", (lambda n: lambda s_: s_.dma_start(
                out=wstg[:, 0:4096].rearrange("p (k f) -> p k f", f=512),
                in_=win_d[:, n * 512:(n + 1) * 512].rearrange("(k p) f -> p k f", p=128)))(n),
                w=["wstg"], dma=True)
            A("vector", lambda v: v.tensor_copy(out=wchk[:], in_=wstg[:, 0:4096]),
              r=["wstg"], w=["wchk"])
            for i in range(NT):
                for c in range(8):
                    A("tensor", (lambda i, n, c: lambda t: t.matmul(
                        out=ps_m[:], lhsT=lnbTa[:, (i * 8 + c) * 128:(i * 8 + c + 1) * 128],
                        rhs=wchk[:, c * 512:(c + 1) * 512],
                        start=(c == 0), stop=(c == 7)))(i, n, c),
                        r=["lnbTa", "wchk"], w=["ps_m"])
                A("vector", (lambda n: lambda v: v.tensor_copy(
                    out=evt[:], in_=ps_m[:]))(n), r=["ps_m"], w=["evt"])
                dst = xin_t[i][:, (n % 4) * 512:(n % 4 + 1) * 512] if n < 4 else \
                      z_t[i][:, (n - 4) * 512:(n - 3) * 512]
                A("sync", (lambda i, n, dst: lambda s_: s_.dma_start(
                    out=dst, in_=evt[:]))(i, n, dst), r=["evt"],
                    w=[f"xin{i}_{n}" if n < 4 else f"z{i}_{n-4}"], dma=True)
        xin_deps = [f"xin{i}_{n}" for i in range(NT) for n in range(4)]

        A("gpsimd", lambda g: g.dma_start(
            out=hl_in, in_=xin_d[T - 3:T, :].rearrange("a (b c) -> (a b) c", c=128)),
            deps=[sch.last_w.get(k) for k in xin_deps], w=["hl_in"], dma=True)
        A("gpsimd", lambda g: g.collective_compute(
            "AllGather", ALU.bypass, replica_groups=[list(range(NCORE))],
            ins=[hl_in.opt()], outs=[hl_out.opt()]), r=["hl_in"], w=["hl_out"])
        A("vector", lambda v: v.memset(halo3[0:3, :], 0.0), w=["halo3"])
        for m in range(8):
            A("gpsimd", (lambda m: lambda g: g.dma_start(
                out=halo3t[0:3, :], in_=hl_out.rearrange("(m a) b -> m a b", a=48)[m]
                .rearrange("a b -> (a b)").rearrange("(p f) -> p f", p=3)))(m),
                r=["hl_out"], w=["halo3t"], dma=True)
            A("vector", (lambda m: lambda v: v.scalar_tensor_tensor(
                out=halo3[0:3, :], in0=halo3t[0:3, :], scalar=hsel[0:3, m:m + 1],
                in1=halo3[0:3, :], op0=ALU.mult, op1=ALU.add))(m),
                r=["halo3t", "hsel_s", "halo3"], w=["halo3"])

        xct_t = xct_d.rearrange("(b p) t -> b p t", p=128)
        for db in range(NDB):
            A("tensor", (lambda db: lambda t: t.transpose(
                out=ps_t[0:128, 0:3], in_=halo3[0:3, db * 128:(db + 1) * 128],
                identity=identf[0:3, 0:3]))(db), r=["halo3", "identf_s"], w=["ps_tf"])
            A("vector", lambda v: v.tensor_copy(out=xinb[:, 0:3], in_=ps_t[0:128, 0:3]),
              r=["ps_tf"], w=["xinb"])
            for i in range(NT):
                A("sync", (lambda db, i: lambda s_: s_.dma_start(
                    out=evt[:, 0:128],
                    in_=xin_t[i][:, db * 128:(db + 1) * 128]))(db, i),
                    deps=[sch.last_w.get(f"xin{i}_{db // 4}")], w=["evt"], dma=True)
                A("tensor", lambda t: t.transpose(
                    out=ps_t[:, 0:128], in_=evt[:, 0:128], identity=identf[:]),
                    r=["evt", "identf_s"], w=["ps_tf"])
                A("vector", (lambda i: lambda v: v.tensor_copy(
                    out=xinb[:, 3 + i * 128:3 + (i + 1) * 128], in_=ps_t[:, 0:128]))(i),
                    r=["ps_tf"], w=["xinb"])
            A("vector", (lambda db: lambda v: v.tensor_scalar(
                out=xcb[:], in0=xinb[:, 0:T], scalar1=convw[:, db * 4:db * 4 + 1],
                scalar2=None, op0=ALU.mult))(db), r=["xinb", "convw_s"], w=["xcb"])
            for wv in range(1, 4):
                A("vector", (lambda db, wv: lambda v: v.scalar_tensor_tensor(
                    out=xcb[:], in0=xinb[:, wv:T + wv],
                    scalar=convw[:, db * 4 + wv:db * 4 + wv + 1], in1=xcb[:],
                    op0=ALU.mult, op1=ALU.add))(db, wv),
                    r=["xinb", "convw_s", "xcb"], w=["xcb"])
            A("vector", (lambda db: lambda v: v.tensor_scalar(
                out=xcb[:], in0=xcb[:], scalar1=convb[:, db:db + 1],
                scalar2=None, op0=ALU.add))(db), r=["xcb", "convb_s"], w=["xcb"])
            A("scalar", lambda s_: s_.activation(out=xcb[:], in_=xcb[:], func=AF.Silu),
              r=["xcb"], w=["xcb"])
            if debug and db == 1:
                A("sync", lambda s_: s_.dma_start(out=dbg_d[:, 0:2051], in_=xinb[:]),
                  r=["xinb"], dma=True)
                A("sync", lambda s_: s_.dma_start(out=dbg_d[:, 3072:5120], in_=xcb[:]),
                  r=["xcb"], dma=True)
            A("sync", (lambda db: lambda s_: s_.dma_start(
                out=xct_t[db], in_=xcb[:]))(db), r=["xcb"], w=[f"xct{db}"], dma=True)
            A("vector", lambda v: v.tensor_copy(out=xcbb[:], in_=xcb[:]),
              r=["xcb"], w=["xcbb"])
            A("sync", (lambda db: lambda s_: s_.dma_start(
                out=wstg[:, 0:33], in_=wxp_d.rearrange("(b p) f -> b p f", p=128)[db]))(db),
                w=["wstg"], dma=True)
            A("vector", lambda v: v.tensor_copy(out=wxpb[:], in_=wstg[:, 0:33]),
              r=["wstg"], w=["wxpb"])
            for c in range(4):
                A("tensor", (lambda db, c: lambda t: t.matmul(
                    out=ps_u[0:33, c * 512:(c + 1) * 512], lhsT=wxpb[:],
                    rhs=xcbb[:, c * 512:(c + 1) * 512],
                    start=(db == 0), stop=(db == NDB - 1)))(db, c),
                    r=["wxpb", "xcbb"], w=[f"ps_u{c}"])
        A("vector", lambda v: v.tensor_copy(out=xssm[:], in_=ps_u[0:33, :]),
          r=["ps_u0", "ps_u1", "ps_u2", "ps_u3"], w=["xssm"])

        def build_delta(db):
            for c in range(4):
                A("tensor", (lambda c: lambda t: t.matmul(
                    out=ps_m[:], lhsT=selbc[:, 0:128],
                    rhs=xssm[:, c * 512:(c + 1) * 512],
                    start=True, stop=True))(c), r=["xssm", "selbc_s"], w=["ps_m"])
                A("vector", (lambda db, c: lambda v: v.tensor_scalar(
                    out=delta[:, c * 512:(c + 1) * 512], in0=ps_m[:],
                    scalar1=wdt[:, db:db + 1], scalar2=bdt[:, db:db + 1],
                    op0=ALU.mult, op1=ALU.add))(db, c),
                    r=["ps_m", "wdt_s", "bdt_s"], w=["delta", "halo3t"])
            A("scalar", lambda s_: s_.activation(out=dab[:], in_=delta[:], func=AF.Abs),
              r=["delta"], w=["dab"])
            A("scalar", lambda s_: s_.activation(out=dab[:], in_=dab[:], func=AF.Exp,
              scale=-1.0), r=["dab"], w=["dab"])
            A("scalar", lambda s_: s_.activation(out=dab[:], in_=dab[:], func=AF.Ln,
              bias=onecol[:, 0:1]), r=["dab", "onecol"], w=["dab"])
            A("vector", lambda v: v.tensor_relu(out=delta[:], in_=delta[:]),
              r=["delta"], w=["delta"])
            A("vector", lambda v: v.tensor_add(out=delta[:], in0=delta[:], in1=dab[:]),
              r=["delta", "dab"], w=["delta"])
            A("vector", lambda v: v.tensor_tensor_scan(out=dcum[:],
              data0=onecol[:, 0:1].to_broadcast([128, T]), data1=delta[:],
              initial=0.0, op0=ALU.mult, op1=ALU.add),
              r=["onecol", "delta"], w=["dcum", "halo3"])

        y_t = y_d.rearrange("(b p) t -> b p t", p=128)
        for db in range(NDB):
            build_delta(db)
            A("sync", (lambda db: lambda s_: s_.dma_start(
                out=xcb[:], in_=xct_t[db]))(db), deps=[sch.last_w.get(f"xct{db}")],
                w=["xcb"], dma=True)
            A("vector", lambda v: v.tensor_mul(out=wdx[:], in0=delta[:], in1=xcb[:]),
              r=["delta", "xcb"], w=["wdx"])
            A("vector", (lambda db: lambda v: v.tensor_scalar(
                out=pfb[:, db * 16:(db + 1) * 16], in0=acol[:, db * 16:(db + 1) * 16],
                scalar1=dcum[:, T - 1:T], scalar2=None, op0=ALU.mult))(db),
                r=["acol_s", "dcum"], w=["pfb"])
            A("scalar", (lambda db: lambda s_: s_.activation(
                out=pfb[:, db * 16:(db + 1) * 16], in_=pfb[:, db * 16:(db + 1) * 16],
                func=AF.Exp))(db), r=["pfb"], w=["pfb"])
            A("vector", lambda v: v.memset(ybuf[:], 0.0), w=["ybuf"])
            for n in range(NST):
                jj = db * 16 + n
                A("scalar", (lambda jj: lambda s_: s_.activation(
                    out=dab[:], in_=delta[:], func=AF.Exp,
                    scale=acol[:, jj:jj + 1]))(jj),
                    r=["delta", "acol_s"], w=["dab"])
                for c in range(4):
                    A("tensor", (lambda n, c: lambda t: t.matmul(
                        out=ps_u[:, c * 512:(c + 1) * 512],
                        lhsT=selbc[:, (1 + n) * 128:(2 + n) * 128],
                        rhs=xssm[:, c * 512:(c + 1) * 512],
                        start=True, stop=True))(n, c), r=["xssm", "selbc_s"],
                        w=[f"ps_u{c}"])
                    A("vector", (lambda c: lambda v: v.tensor_tensor(
                        out=ubuf[:, c * 512:(c + 1) * 512],
                        in0=wdx[:, c * 512:(c + 1) * 512],
                        in1=ps_u[:, c * 512:(c + 1) * 512], op=ALU.mult))(c),
                        r=["wdx", f"ps_u{c}"], w=["ubuf"])
                A("vector", lambda v: v.tensor_tensor_scan(out=hbuf[:], data0=dab[:],
                  data1=ubuf[:], initial=0.0, op0=ALU.mult, op1=ALU.add),
                  r=["dab", "ubuf"], w=["hbuf"])
                A("vector", (lambda jj: lambda v: v.tensor_copy(
                    out=sigm[:, jj:jj + 1], in_=hbuf[:, T - 1:T]))(jj),
                    r=["hbuf"], w=["sigm"])
                for c in range(4):
                    A("tensor", (lambda n, c: lambda t: t.matmul(
                        out=ps_u[:, c * 512:(c + 1) * 512],
                        lhsT=selbc[:, (17 + n) * 128:(18 + n) * 128],
                        rhs=xssm[:, c * 512:(c + 1) * 512],
                        start=True, stop=True))(n, c), r=["xssm", "selbc_s"],
                        w=[f"ps_u{c}"])
                    A("vector", (lambda c: lambda v: v.tensor_tensor(
                        out=ubuf[:, c * 512:(c + 1) * 512],
                        in0=hbuf[:, c * 512:(c + 1) * 512],
                        in1=ps_u[:, c * 512:(c + 1) * 512], op=ALU.mult))(c),
                        r=["hbuf", f"ps_u{c}"], w=["ubuf"])
                A("vector", lambda v: v.tensor_add(out=ybuf[:], in0=ybuf[:],
                  in1=ubuf[:]), r=["ybuf", "ubuf"], w=["ybuf"])
                if debug >= 4 and db == 0 and n == 0:
                    pairs = {4: ("delta", delta), 5: ("dab", dab), 6: ("ubuf", ubuf),
                             7: ("hbuf", hbuf), 8: ("xcb", xcb)}
                    if debug in pairs:
                        kk, tt = pairs[debug]
                        A("sync", (lambda tt: lambda s_: s_.dma_start(
                            out=dbg_d[:, 0:2048], in_=tt[:]))(tt), r=[kk], dma=True)
                    elif debug == 9:
                        A("sync", lambda s_: s_.dma_start(out=dbg_d[0:33, 0:2048], in_=xssm[:]),
                          r=["xssm"], dma=True)
            A("vector", (lambda db: lambda v: v.scalar_tensor_tensor(
                out=ybuf[:], in0=xcb[:], scalar=dpar[:, db:db + 1], in1=ybuf[:],
                op0=ALU.mult, op1=ALU.add))(db),
                r=["xcb", "dpar_s", "ybuf"], w=["ybuf"])
            A("sync", (lambda db: lambda s_: s_.dma_start(
                out=y_t[db], in_=ybuf[:]))(db), r=["ybuf"], w=[f"y{db}"], dma=True)

        A("gpsimd", lambda g: g.dma_start(
            out=st_in[0:256, :].rearrange("(a b) c -> a (b c)", b=2),
            in_=sigm[:]), r=["sigm"], w=["st_in"], dma=True)
        A("gpsimd", lambda g: g.dma_start(
            out=st_in[256:512, :].rearrange("(a b) c -> a (b c)", b=2),
            in_=pfb[:]), r=["pfb"], w=["st_in"], dma=True)
        A("gpsimd", lambda g: g.collective_compute(
            "AllGather", ALU.bypass, replica_groups=[list(range(NCORE))],
            ins=[st_in.opt()], outs=[st_out.opt()]), r=["st_in"], w=["st_out"])
        A("vector", lambda v: v.memset(h0m[:], 0.0), w=["h0m"])
        A("vector", lambda v: v.memset(sS[:], 0.0), w=["sS"])
        for m in range(8):
            A("gpsimd", (lambda m: lambda g: g.dma_start(
                out=agb[:], in_=st_out[m * 512:m * 512 + 256, :]
                .rearrange("(a b) c -> a (b c)", b=2)))(m),
                r=["st_out"], w=["agb"], dma=True)
            A("gpsimd", (lambda m: lambda g: g.dma_start(
                out=agb2[:], in_=st_out[m * 512 + 256:(m + 1) * 512, :]
                .rearrange("(a b) c -> a (b c)", b=2)))(m),
                r=["st_out"], w=["agb2"], dma=True)
            A("vector", lambda v: v.tensor_mul(out=sS[:], in0=sS[:], in1=agb2[:]),
              r=["sS", "agb2"], w=["sS"])
            A("vector", lambda v: v.tensor_add(out=sS[:], in0=sS[:], in1=agb[:]),
              r=["sS", "agb"], w=["sS"])
            A("vector", (lambda m: lambda v: v.scalar_tensor_tensor(
                out=h0m[:], in0=sS[:], scalar=capg[:, m:m + 1], in1=h0m[:],
                op0=ALU.mult, op1=ALU.add))(m), r=["sS", "capg_s", "h0m"], w=["h0m"])

        for db in range(NDB):
            build_delta(db)
            A("sync", (lambda db: lambda s_: s_.dma_start(
                out=ybuf[:], in_=y_t[db]))(db), deps=[sch.last_w.get(f"y{db}")],
                w=["ybuf"], dma=True)
            for n in range(NST):
                jj = db * 16 + n
                A("scalar", (lambda jj: lambda s_: s_.activation(
                    out=dab[:], in_=dcum[:], func=AF.Exp,
                    scale=acol[:, jj:jj + 1]))(jj),
                    r=["dcum", "acol_s"], w=["dab"])
                A("vector", (lambda jj: lambda v: v.tensor_scalar(
                    out=dab[:], in0=dab[:], scalar1=h0m[:, jj:jj + 1],
                    scalar2=None, op0=ALU.mult))(jj), r=["dab", "h0m"], w=["dab"])
                for c in range(4):
                    A("tensor", (lambda n, c: lambda t: t.matmul(
                        out=ps_u[:, c * 512:(c + 1) * 512],
                        lhsT=selbc[:, (17 + n) * 128:(18 + n) * 128],
                        rhs=xssm[:, c * 512:(c + 1) * 512],
                        start=True, stop=True))(n, c), r=["xssm", "selbc_s"],
                        w=[f"ps_u{c}"])
                    A("vector", (lambda c: lambda v: v.tensor_tensor(
                        out=ubuf[:, c * 512:(c + 1) * 512],
                        in0=dab[:, c * 512:(c + 1) * 512],
                        in1=ps_u[:, c * 512:(c + 1) * 512], op=ALU.mult))(c),
                        r=["dab", f"ps_u{c}"], w=["ubuf"])
                A("vector", lambda v: v.tensor_add(out=ybuf[:], in0=ybuf[:],
                  in1=ubuf[:]), r=["ybuf", "ubuf"], w=["ybuf"])
            if debug in (2, 3) and db == 0:
                A("sync", lambda s_: s_.dma_start(out=dbg_d[:, 4096:6144], in_=ybuf[:]),
                  r=["ybuf"], dma=True)
            A("sync", (lambda db: lambda s_: s_.dma_start(
                out=y_t[db], in_=ybuf[:]))(db), r=["ybuf"], w=[f"y{db}"], dma=True)

        ygb_t = ygb_d.rearrange("(b p) t -> b p t", p=128)
        for db in range(NDB):
            A("sync", (lambda db: lambda s_: s_.dma_start(
                out=ybuf[:], in_=y_t[db]))(db), deps=[sch.last_w.get(f"y{db}")],
                w=["ybuf"], dma=True)
            for i in range(NT):
                A("sync", (lambda db, i: lambda s_: s_.dma_start(
                    out=evt[:, 0:128], in_=z_t[i][:, db * 128:(db + 1) * 128]))(db, i),
                    deps=[sch.last_w.get(f"z{i}_{db // 4}")], w=["evt"], dma=True)
                A("tensor", lambda t: t.transpose(
                    out=ps_t[:, 0:128], in_=evt[:, 0:128], identity=identf[:]),
                    r=["evt", "identf_s"], w=["ps_tf"])
                A("vector", (lambda i: lambda v: v.tensor_copy(
                    out=ubuf[:, i * 128:(i + 1) * 128], in_=ps_t[:, 0:128]))(i),
                    r=["ps_tf"], w=["ubuf"])
            A("scalar", lambda s_: s_.activation(out=ubuf[:], in_=ubuf[:], func=AF.Silu),
              r=["ubuf"], w=["ubuf"])
            A("vector", lambda v: v.tensor_mul(out=ybuf[:], in0=ybuf[:], in1=ubuf[:]),
              r=["ybuf", "ubuf"], w=["ybuf"])
            if debug in (2, 3) and db == 0:
                A("sync", lambda s_: s_.dma_start(out=dbg_d[:, 6144:8192], in_=ybuf[:]),
                  r=["ybuf"], dma=True)
                A("sync", lambda s_: s_.dma_start(out=dbg_d[:, 0:2048], in_=ubuf[:]),
                  r=["ubuf"], dma=True)
            A("vector", lambda v: v.tensor_copy(out=xcbb[:], in_=ybuf[:]),
              r=["ybuf"], w=["xcbb"])
            A("sync", (lambda db: lambda s_: s_.dma_start(
                out=ygb_t[db], in_=xcbb[:]))(db), r=["xcbb"], w=[f"ygb{db}"], dma=True)

        for db in range(NDB):
            A("sync", (lambda db: lambda s_: s_.dma_start(
                out=wstg[:, 0:D], in_=wout_d.rearrange("(b p) f -> b p f", p=128)[db]))(db),
                w=["wstg"], dma=True)
            A("vector", (lambda db: lambda v: v.tensor_copy(
                out=lnbTa[:, db * D:(db + 1) * D], in_=wstg[:, 0:D]))(db),
                r=["wstg"], w=["lnbTa"])
        for i in range(NT):
            A("sync", (lambda i: lambda s_: s_.dma_start(
                out=xtile[:], in_=xc_t[i]))(i),
                deps=[sch.last_w.get(f"xcp{i}")], w=["xtile"], dma=True)
            for db in range(NDB):
                A("sync", (lambda i, db: lambda s_: s_.dma_start(
                    out=xcbb[:, db * 128:(db + 1) * 128],
                    in_=ygb_t[db][:, i * 128:(i + 1) * 128]))(i, db),
                    deps=[sch.last_w.get(f"ygb{db}")], w=["xcbb"], dma=True)
            for n in range(2):
                for db in range(NDB):
                    A("tensor", (lambda i, n, db: lambda t: t.matmul(
                        out=ps_m[:], lhsT=xcbb[:, db * 128:(db + 1) * 128],
                        rhs=lnbTa[:, db * D + n * 512:db * D + (n + 1) * 512],
                        start=(db == 0), stop=(db == NDB - 1)))(i, n, db),
                        r=["xcbb", "lnbTa"], w=["ps_m"])
                A("vector", (lambda i, n: lambda v: v.tensor_add(
                    out=outt[:], in0=ps_m[:],
                    in1=xtile[:, n * 512:(n + 1) * 512]))(i, n),
                    r=["ps_m", "xtile"], w=["outt"])
                A("sync", (lambda i, n: lambda s_: s_.dma_start(
                    out=xo_d.rearrange("(m p) d -> m p d", p=128)[i][:, n * 512:(n + 1) * 512],
                    in_=outt[:]))(i, n), r=["outt"], w=[f"xo{i}_{n}"], dma=True)

        make_block(nc, ctx, sch, final_eng="gpsimd")
    return nc


def mamba_host_layout(conv_w, conv_b, w_dt, b_dt, A_log, D_param, core_id):
    p = np.arange(128)
    db = np.arange(NDB)
    out = {}
    cw = np.asarray(conv_w).reshape(DI, 4)
    out["convw"] = _f32(cw[(db[None, :, None] * 128 + p[:, None, None]),
                           np.arange(4)[None, None, :]].reshape(128, NDB * 4))
    for name, vec in (("convb", conv_b), ("wdt", w_dt), ("bdt", b_dt), ("dpar", D_param)):
        out[name] = _f32(np.asarray(vec).reshape(NDB, 128).T)
    al = np.asarray(A_log)
    out["alog"] = _f32(al.reshape(NDB, 128, NST).transpose(1, 0, 2).reshape(128, NDB * NST))
    hsel = np.zeros((128, 8), np.float32)
    capg = np.zeros((128, 8), np.float32)
    if core_id > 0:
        hsel[:, core_id - 1] = 1.0
        capg[:, core_id - 1] = 1.0
    out["hsel"] = hsel
    out["capg"] = capg
    k = np.arange(33)
    j = np.arange(33)
    sel = (k[:, None, None] == j[None, :, None]) * np.ones((1, 1, 128))
    out["selbc"] = _f32(sel.reshape(33, 33 * 128))
    return out


_MAMBA_CACHE = {}


def run_mamba(x_full, W_in, conv_w, conv_b, W_xproj, w_dt, b_dt, A_log, D_param, W_out):
    import os
    dbg = int(os.environ.get("MAMBA_DEBUG", "0"))
    if "nc" not in _MAMBA_CACHE:
        _MAMBA_CACHE["nc"] = build_mamba(debug=dbg)
    nc = _MAMBA_CACHE["nc"]
    shards = np.split(_f32(x_full), NCORE, axis=0)
    in_maps = []
    for c in range(NCORE):
        m = {"x": shards[c], "win": _f32(W_in), "wxp": _f32(W_xproj),
             "wout": _f32(W_out), "identf": _f32(np.eye(128)),
             "identb": _bf16(np.eye(128)),
             "onesrow": _f32(np.ones((1, 128)))}
        m.update(mamba_host_layout(conv_w, conv_b, w_dt, b_dt, A_log, D_param, c))
        in_maps.append(m)
    res = run_bass_kernel_spmd(nc, in_maps, core_ids=list(range(NCORE)))
    if dbg:
        np.save("/tmp/mamba_dbg.npy", res.results[0]["dbg"])
    return np.concatenate([r["xo"] for r in res.results], axis=0)




def kernel(**inputs):
    x = _f32(inputs["L0_attn_x"]).reshape(16384, 1024)
    x = run_ffn0(x, inputs["L0_ffn_w_gate"], inputs["L0_ffn_w_up"],
                 inputs["L0_ffn_w_down"])
    x = run_attn(x, inputs["L0_attn_W_qkv"], inputs["L0_attn_W_o"],
                 inputs["L0_attn_W_LTM"], inputs["L0_attn_V_gs"],
                 inputs["L0_attn_V_T0"], inputs["L0_attn_beta_tau"],
                 inputs["L0_attn_beta_gm"], inputs["L0_attn_C_ch"],
                 inputs["L0_attn_gamma"], inputs["L0_attn_alpha_ppd"],
                 inputs["L0_attn_IC_threshold"])
    x = run_ffn0(x, inputs["L0_ffn_w_gate"], inputs["L0_ffn_w_up"],
                 inputs["L0_ffn_w_down"])
    x = run_moe(x, inputs["L1_ffn_gate_w"], inputs["L1_ffn_w1"],
                inputs["L1_ffn_w2"])
    x = run_mamba(x, inputs["L1_attn_W_in"], inputs["L1_attn_conv_w"],
                  inputs["L1_attn_conv_b"], inputs["L1_attn_W_xproj"],
                  inputs["L1_attn_w_dt"], inputs["L1_attn_b_dt"],
                  inputs["L1_attn_A_log"], inputs["L1_attn_D_param"],
                  inputs["L1_attn_W_out"])
    x = run_moe(x, inputs["L1_ffn_gate_w"], inputs["L1_ffn_w1"],
                inputs["L1_ffn_w2"])
    return _f32(x).reshape(1, 16384, 1024)


NE = 16
CL = 384
CAP = 2560
SENT = NE * CL


def build_moe(debug=False):
    nc = bass.Bass("TRN2", target_bir_lowering=False, debug=False)
    x_d = nc.dram_tensor("x", [T, D], F32, kind="ExternalInput").ap()
    gwT_d = nc.dram_tensor("gwT", [D, NE], F32, kind="ExternalInput").ap()
    w1_d = nc.dram_tensor("w1", [NE * D, 2048], F32, kind="ExternalInput").ap()
    w2_d = nc.dram_tensor("w2", [NE * 2048, D], F32, kind="ExternalInput").ap()
    identb_d = nc.dram_tensor("identb", [128, 128], BF16, kind="ExternalInput").ap()
    identf_d = nc.dram_tensor("identf", [128, 128], F32, kind="ExternalInput").ap()
    iotaE_d = nc.dram_tensor("iotaE", [128, NE], F32, kind="ExternalInput").ap()
    iotaEc_d = nc.dram_tensor("iotaEc", [NE, 1], F32, kind="ExternalInput").ap()
    tokid_d = nc.dram_tensor("tokid", [128, NT], F32, kind="ExternalInput").ap()
    prevm_d = nc.dram_tensor("prevm", [NE, 8], F32, kind="ExternalInput").ap()
    xo_d = nc.dram_tensor("xo", [T, D], F32, kind="ExternalOutput").ap()
    xnb_d = nc.dram_tensor("xnb_scratch", [T, D], BF16).ap()
    xcp_d = nc.dram_tensor("xcp_scratch", [T, D], F32).ap()
    inv_d = nc.dram_tensor("inv_scratch", [SENT + 128, 1], F32).ap()
    oub_d = nc.dram_tensor("oub_scratch", [SENT + 128, D], F32,
                           kind="ExternalOutput").ap()
    if debug:
        dbgm_d = nc.dram_tensor("dbgm", [128, 8192], F32, kind="ExternalOutput").ap()
    cc_in = nc.dram_tensor("cc_in", [NE, 128], F32).ap()
    cc_out = nc.dram_tensor("cc_out", [8 * NE, 128], F32).ap()

    with ExitStack() as ctx:
        e = ctx.enter_context
        identb = e(nc.sbuf_tensor("identb_s", [128, 128], BF16))
        identf = e(nc.sbuf_tensor("identf_s", [128, 128], F32))
        iotaE = e(nc.sbuf_tensor("iotaE_s", [128, NE], F32))
        iotaEc = e(nc.sbuf_tensor("iotaEc_s", [NE, 1], F32))
        tokid = e(nc.sbuf_tensor("tokid_s", [128, NT], F32))
        prevm = e(nc.sbuf_tensor("prevm_s", [NE, 8], F32))
        epscol = e(nc.sbuf_tensor("epscol", [128, 1], F32))
        xtile = e(nc.sbuf_tensor("xtile", [128, D], F32))
        lnsc = e(nc.sbuf_tensor("lnsc", [128, D], F32))
        lnb = e(nc.sbuf_tensor("lnb", [128, D], BF16))
        lnbT = e(nc.sbuf_tensor("lnbT", [128, 8 * 128], BF16))
        gwb = e(nc.sbuf_tensor("gwb", [128, 8 * NE], F32))
        cols = e(nc.sbuf_tensor("cols", [128, 16], F32))
        probs = e(nc.sbuf_tensor("probs", [128, NE], F32))
        ptmp = e(nc.sbuf_tensor("ptmp", [128, NE], F32))
        sel = e(nc.sbuf_tensor("sel", [128, 8 * NT], F32))
        idxr1 = e(nc.sbuf_tensor("idxr1", [1, T], F32))
        idxr2 = e(nc.sbuf_tensor("idxr2", [1, T], F32))
        lrr = [e(nc.sbuf_tensor(f"lrr{j}", [1, T], F32)) for j in range(4)]
        rep = e(nc.sbuf_tensor("rep", [NE, T], F32))
        ohb = e(nc.sbuf_tensor("ohb", [NE, T], F32))
        oh1 = e(nc.sbuf_tensor("oh1", [NE, T], F32))
        cums = e(nc.sbuf_tensor("cums", [NE, T], F32))
        rtmp = e(nc.sbuf_tensor("rtmp", [NE, T], F32))

        counts = e(nc.sbuf_tensor("counts", [NE, 16], F32))
        onecol = e(nc.sbuf_tensor("onecol", [NE, 1], F32))
        onesr = e(nc.sbuf_tensor("onesr", [1, NE], F32))
        slotc = e(nc.sbuf_tensor("slotc", [128, 4], F32))
        sloti = e(nc.sbuf_tensor("sloti", [128, 4], I32))
        tokstg = e(nc.sbuf_tensor("tokstg", [128, 1], F32))
        idxstg = e(nc.sbuf_tensor("idxstg", [128, 1], I32))
        zcol = e(nc.sbuf_tensor("zcol", [128, D], F32))
        xg = e(nc.sbuf_tensor("xg", [128, D], BF16))
        xgT = e(nc.sbuf_tensor("xgT", [128, 8 * 128], BF16))
        w1b = e(nc.sbuf_tensor("w1b", [128, 8 * 2048], BF16))
        w2b = e(nc.sbuf_tensor("w2b", [128, 16 * D], BF16))
        wstg = e(nc.sbuf_tensor("wstg", [128, 2048], F32))
        h1b = e(nc.sbuf_tensor("h1b", [128, 2048], BF16))
        h1T = e(nc.sbuf_tensor("h1T", [128, 16 * 128], BF16))
        opc = e(nc.sbuf_tensor("opc", [128, D], F32))
        g1 = e(nc.sbuf_tensor("g1", [128, D], F32))
        outt = e(nc.sbuf_tensor("outt", [128, D], F32))
        ps_m = e(nc.psum_tensor("ps_m", [128, 512], F32))
        ps_n = e(nc.psum_tensor("ps_n", [128, 512], F32))
        ps_t = e(nc.psum_tensor("ps_tf", [128, 512], F32))
        ps_tb = e(nc.psum_tensor("ps_tbb", [128, 1024], BF16))

        sch = Sched(nc, ctx)
        A = sch.add

        for (ts_, td_, kk) in ((identb, identb_d, "identb_s"), (identf, identf_d, "identf_s"),
                               (iotaE, iotaE_d, "iotaE_s"), (iotaEc, iotaEc_d, "iotaEc_s"),
                               (tokid, tokid_d, "tokid_s"), (prevm, prevm_d, "prevm_s")):
            A("sync", (lambda ts_, td_: lambda s_: s_.dma_start(out=ts_[:], in_=td_))(ts_, td_),
              w=[kk], dma=True)
        A("vector", lambda v: v.memset(epscol[:], LN_EPS), w=["epscol"])
        A("vector", lambda v: v.memset(onecol[:], 1.0), w=["onecol"])
        A("vector", lambda v: v.memset(onesr[:], 1.0), w=["onesr"])
        A("vector", lambda v: v.memset(zcol[:], 0.0), w=["zcol"])
        A("sync", lambda s_: s_.dma_start(out=oub_d[SENT:SENT + 1, :],
          in_=zcol[0:1, :]), r=["zcol"], w=["oub_sent"], dma=True)
        for r in range((SENT + 128) // 128):
            A("sync", (lambda r: lambda s_: s_.dma_start(
                out=inv_d[r * 128:(r + 1) * 128, :],
                in_=zcol[:, 0:1]))(r), r=["zcol"], w=["inv"], dma=True)

        A("sync", lambda s_: s_.dma_start(
            out=gwb[:].rearrange("p (k f) -> p k f", f=NE),
            in_=gwT_d.rearrange("(k p) f -> p k f", p=128)), w=["gwb"], dma=True)

        x_t = x_d.rearrange("(n p) d -> n p d", p=128)
        xnb_t = xnb_d.rearrange("(n p) d -> n p d", p=128)
        xcp_t = xcp_d.rearrange("(n p) d -> n p d", p=128)

        for i in range(NT):
            A("sync", (lambda i: lambda s_: s_.dma_start(
                out=xtile[:], in_=x_t[i]))(i), w=["xtile"], dma=True)
            A("sync", (lambda i: lambda s_: s_.dma_start(
                out=xcp_t[i], in_=xtile[:]))(i), r=["xtile"], w=[f"xcp{i}"], dma=True)
            emit_ln(sch, xtile[:], "xtile", outt[:], "outt", lnsc[:], cols, epscol[:, 0:1])
            A("vector", lambda v: v.tensor_copy(out=lnb[:], in_=outt[:]),
              r=["outt"], w=["lnb"])
            A("sync", (lambda i: lambda s_: s_.dma_start(
                out=xnb_t[i], in_=lnb[:]))(i), r=["lnb"], w=[f"xnb{i}"], dma=True)
            for c in range(8):
                emit_transpose(sch, outt[:, c * 128:(c + 1) * 128], "outt",
                               opc[:, c * 128:(c + 1) * 128], "opc", ps_t, identf,
                               ps_key="ps_tf", ident_key="identf_s")
            for c in range(8):
                A("tensor", (lambda c: lambda t: t.matmul(
                    out=ps_m[:, 0:NE], lhsT=opc[:, c * 128:(c + 1) * 128],
                    rhs=gwb[:, c * NE:(c + 1) * NE],
                    start=(c == 0), stop=(c == 7)))(c),
                    r=["opc", "gwb"], w=["ps_m"])
            A("scalar", lambda s_: s_.activation(out=probs[:], in_=ps_m[:, 0:NE],
              func=AF.Sigmoid), r=["ps_m"], w=["probs"])
            A("vector", lambda v: v.tensor_reduce(out=cols[:, 0:1], in_=probs[:],
              axis=AX.X, op=ALU.max), r=["probs"], w=["c0"])
            A("vector", lambda v: v.tensor_scalar(out=ptmp[:], in0=probs[:],
              scalar1=cols[:, 0:1], scalar2=None, op0=ALU.is_ge), r=["probs", "c0"], w=["ptmp"])
            A("vector", lambda v: v.scalar_tensor_tensor(out=ptmp[:], in0=ptmp[:],
              scalar=-1000.0, in1=iotaE[:], op0=ALU.mult, op1=ALU.add),
              r=["ptmp", "iotaE_s"], w=["ptmp"])
            A("vector", lambda v: v.tensor_reduce(out=cols[:, 1:2], in_=ptmp[:],
              axis=AX.X, op=ALU.min), r=["ptmp"], w=["c1"])
            A("vector", lambda v: v.tensor_scalar(out=cols[:, 1:2], in0=cols[:, 1:2],
              scalar1=1000.0, scalar2=None, op0=ALU.add), r=["c1"], w=["c1"])
            A("vector", lambda v: v.tensor_scalar(out=ptmp[:], in0=iotaE[:],
              scalar1=cols[:, 1:2], scalar2=None, op0=ALU.is_equal),
              r=["iotaE_s", "c1"], w=["ptmp"])
            A("vector", lambda v: v.scalar_tensor_tensor(out=probs[:], in0=ptmp[:],
              scalar=-1000.0, in1=probs[:], op0=ALU.mult, op1=ALU.add),
              r=["ptmp", "probs"], w=["probs"])
            A("vector", lambda v: v.tensor_reduce(out=cols[:, 2:3], in_=probs[:],
              axis=AX.X, op=ALU.max), r=["probs"], w=["c2"])
            A("vector", lambda v: v.tensor_scalar(out=ptmp[:], in0=probs[:],
              scalar1=cols[:, 2:3], scalar2=None, op0=ALU.is_ge), r=["probs", "c2"], w=["ptmp"])
            A("vector", lambda v: v.scalar_tensor_tensor(out=ptmp[:], in0=ptmp[:],
              scalar=-1000.0, in1=iotaE[:], op0=ALU.mult, op1=ALU.add),
              r=["ptmp", "iotaE_s"], w=["ptmp"])
            A("vector", lambda v: v.tensor_reduce(out=cols[:, 3:4], in_=ptmp[:],
              axis=AX.X, op=ALU.min), r=["ptmp"], w=["c3"])
            A("vector", lambda v: v.tensor_scalar(out=cols[:, 3:4], in0=cols[:, 3:4],
              scalar1=1000.0, scalar2=None, op0=ALU.add), r=["c3"], w=["c3"])
            A("vector", lambda v: v.tensor_add(out=cols[:, 4:5], in0=cols[:, 0:1],
              in1=cols[:, 2:3]), r=["c0", "c2"], w=["c4"])
            A("vector", lambda v: v.tensor_scalar(out=cols[:, 4:5], in0=cols[:, 4:5],
              scalar1=1e-6, scalar2=None, op0=ALU.add), r=["c4"], w=["c4"])
            A("vector", lambda v: v.reciprocal(out=cols[:, 4:5], in_=cols[:, 4:5]),
              r=["c4"], w=["c4"])
            A("vector", lambda v: v.tensor_scalar(out=cols[:, 4:5], in0=cols[:, 4:5],
              scalar1=0.5, scalar2=None, op0=ALU.mult), r=["c4"], w=["c4"])
            A("vector", (lambda i: lambda v: v.tensor_mul(
                out=sel[:, i * 8 + 2:i * 8 + 3], in0=cols[:, 0:1], in1=cols[:, 4:5]))(i),
                r=["c0", "c4"], w=["sel"])
            A("vector", (lambda i: lambda v: v.tensor_mul(
                out=sel[:, i * 8 + 3:i * 8 + 4], in0=cols[:, 2:3], in1=cols[:, 4:5]))(i),
                r=["c2", "c4"], w=["sel"])
            A("vector", (lambda i: lambda v: v.tensor_copy(
                out=sel[:, i * 8 + 0:i * 8 + 1], in_=cols[:, 1:2]))(i), r=["c1"], w=["sel"])
            A("vector", (lambda i: lambda v: v.tensor_copy(
                out=sel[:, i * 8 + 1:i * 8 + 2], in_=cols[:, 3:4]))(i), r=["c3"], w=["sel"])
            for k, dst in ((0, idxr1), (1, idxr2)):
                A("tensor", (lambda i, k: lambda t: t.transpose(
                    out=ps_t[0:1, 0:128], in_=sel[:, i * 8 + k:i * 8 + k + 1],
                    identity=identf[:]))(i, k), r=["sel", "identf_s"], w=["ps_tf"])
                A("vector", (lambda i, dst: lambda v: v.tensor_copy(
                    out=dst[0:1, i * 128:(i + 1) * 128], in_=ps_t[0:1, 0:128]))(i, dst),
                    r=["ps_tf"], w=[_key(dst)])

        for k in range(2):
            srcrow = idxr1 if k == 0 else idxr2
            for c in range(4):
                A("tensor", (lambda srcrow, c: lambda t: t.matmul(
                    out=ps_m[0:NE, :], lhsT=onesr[:],
                    rhs=srcrow[0:1, c * 512:(c + 1) * 512],
                    start=True, stop=True))(srcrow, c), r=[_key(srcrow), "onesr"], w=["ps_m"])
                A("vector", (lambda k, c: lambda v: v.tensor_copy(
                    out=rep[0:NE, c * 512:(c + 1) * 512], in_=ps_m[0:NE, :]))(k, c),
                    r=["ps_m"], w=["rep"])
            A("vector", (lambda k: lambda v: v.tensor_scalar(
                out=(oh1 if k == 0 else ohb)[:, :], in0=rep[:, :],
                scalar1=iotaEc[:, 0:1], scalar2=None, op0=ALU.is_equal))(k),
                r=["rep", "iotaEc_s"], w=["oh1" if k == 0 else "ohb"])
        A("vector", lambda v: v.tensor_add(out=ohb[:], in0=ohb[:], in1=oh1[:]),
          r=["ohb", "oh1"], w=["ohb"])
        A("vector", lambda v: v.tensor_tensor_scan(out=cums[:], data0=onecol[0:NE, 0:1]
          .to_broadcast([NE, T]), data1=ohb[:], initial=0.0, op0=ALU.mult,
          op1=ALU.add), r=["onecol", "ohb"], w=["cums"])
        A("vector", lambda v: v.tensor_copy(out=counts[:, 0:1], in_=cums[:, T - 1:T]),
          r=["cums"], w=["counts"])
        A("vector", lambda v: v.tensor_sub(out=cums[:], in0=cums[:], in1=ohb[:]),
          r=["cums", "ohb"], w=["cums"])
        A("vector", lambda v: v.memset(rtmp[0:NE, 0:128], 0.0), w=["rtmp"])
        A("vector", lambda v: v.tensor_copy(out=rtmp[0:NE, 0:1], in_=counts[:, 0:1]),
          r=["counts"], w=["rtmp"])
        A("gpsimd", lambda g: g.dma_start(out=cc_in, in_=rtmp[0:NE, 0:128]),
          r=["rtmp"], w=["cc_in"], dma=True)
        A("gpsimd", lambda g: g.collective_compute(
            "AllGather", ALU.bypass, replica_groups=[list(range(NCORE))],
            ins=[cc_in.opt()], outs=[cc_out.opt()]), r=["cc_in"], w=["cc_out"])
        for m in range(8):
            A("gpsimd", (lambda m: lambda g: g.dma_start(
                out=counts[:, 2 + m:3 + m],
                in_=cc_out.rearrange("(m e) c -> m e c", e=NE)[m][:, 0:1],
                allow_slow_non_contiguous=True))(m),
                r=["cc_out"], w=["counts"], dma=True)
        A("vector", lambda v: v.memset(counts[:, 1:2], 0.0), r=["counts"], w=["counts"])
        for m in range(8):
            A("vector", (lambda m: lambda v: v.scalar_tensor_tensor(
                out=counts[:, 1:2], in0=counts[:, 2 + m:3 + m],
                scalar=prevm[:, m:m + 1], in1=counts[:, 1:2],
                op0=ALU.mult, op1=ALU.add))(m), r=["counts", "prevm_s"], w=["counts"])
        A("vector", lambda v: v.tensor_scalar(out=rtmp[0:NE, 0:T], in0=cums[:],
          scalar1=counts[:, 1:2], scalar2=None, op0=ALU.add), r=["cums", "counts"], w=["rtmp"])
        A("vector", lambda v: v.tensor_sub(out=ohb[:], in0=ohb[:], in1=oh1[:]),
          r=["ohb", "oh1"], w=["oh2"])
        oh2 = ohb
        A("vector", lambda v: v.tensor_mul(out=rep[:], in0=oh1[:], in1=cums[:]),
          r=["oh1", "cums"], w=["rep"])
        for c in range(4):
            A("tensor", (lambda c: lambda t: t.matmul(
                out=ps_n[0:1, :], lhsT=onecol[0:NE, 0:1],
                rhs=rep[0:NE, c * 512:(c + 1) * 512], start=True, stop=True))(c),
                r=["rep", "onecol"], w=["ps_n"])
            A("vector", (lambda c: lambda v: v.tensor_copy(
                out=lrr[0][0:1, c * 512:(c + 1) * 512], in_=ps_n[0:1, :]))(c),
                r=["ps_n"], w=["lrr0"])
        A("vector", lambda v: v.tensor_mul(out=rep[:], in0=oh1[:], in1=rtmp[0:NE, 0:T]),
          r=["oh1", "rtmp"], w=["rep"])
        for c in range(4):
            A("tensor", (lambda c: lambda t: t.matmul(
                out=ps_n[0:1, :], lhsT=onecol[0:NE, 0:1],
                rhs=rep[0:NE, c * 512:(c + 1) * 512], start=True, stop=True))(c),
                r=["rep", "onecol"], w=["ps_n"])
            A("vector", (lambda c: lambda v: v.tensor_copy(
                out=lrr[2][0:1, c * 512:(c + 1) * 512], in_=ps_n[0:1, :]))(c),
                r=["ps_n"], w=["lrr2"])
        A("vector", lambda v: v.tensor_add(out=cums[:], in0=cums[:], in1=oh1[:]),
          r=["cums", "oh1"], w=["cums"])
        A("vector", lambda v: v.tensor_add(out=rtmp[0:NE, 0:T], in0=rtmp[0:NE, 0:T],
          in1=oh1[:]), r=["rtmp", "oh1"], w=["rtmp"])
        A("vector", lambda v: v.tensor_mul(out=rep[:], in0=oh2[:], in1=cums[:]),
          r=["oh2", "cums"], w=["rep"])
        for c in range(4):
            A("tensor", (lambda c: lambda t: t.matmul(
                out=ps_n[0:1, :], lhsT=onecol[0:NE, 0:1],
                rhs=rep[0:NE, c * 512:(c + 1) * 512], start=True, stop=True))(c),
                r=["rep", "onecol"], w=["ps_n"])
            A("vector", (lambda c: lambda v: v.tensor_copy(
                out=lrr[1][0:1, c * 512:(c + 1) * 512], in_=ps_n[0:1, :]))(c),
                r=["ps_n"], w=["lrr1"])
        A("vector", lambda v: v.tensor_mul(out=rep[:], in0=oh2[:], in1=rtmp[0:NE, 0:T]),
          r=["oh2", "rtmp"], w=["rep"])
        for c in range(4):
            A("tensor", (lambda c: lambda t: t.matmul(
                out=ps_n[0:1, :], lhsT=onecol[0:NE, 0:1],
                rhs=rep[0:NE, c * 512:(c + 1) * 512], start=True, stop=True))(c),
                r=["rep", "onecol"], w=["ps_n"])
            A("vector", (lambda c: lambda v: v.tensor_copy(
                out=lrr[3][0:1, c * 512:(c + 1) * 512], in_=ps_n[0:1, :]))(c),
                r=["ps_n"], w=["lrr3"])
        A("vector", lambda v: v.tensor_scalar(out=lrr[2][:], in0=lrr[2][:],
          scalar1=float(CAP), scalar2=None, op0=ALU.is_lt), r=["lrr2"], w=["lrr2"])
        A("vector", lambda v: v.tensor_scalar(out=lrr[3][:], in0=lrr[3][:],
          scalar1=float(CAP), scalar2=None, op0=ALU.is_lt), r=["lrr3"], w=["lrr3"])
        A("vector", lambda v: v.tensor_scalar(out=rtmp[0:1, 0:T], in0=lrr[0][:],
          scalar1=float(CL), scalar2=None, op0=ALU.is_lt), r=["lrr0"], w=["rtmp"])
        A("vector", lambda v: v.tensor_mul(out=lrr[2][:], in0=lrr[2][:],
          in1=rtmp[0:1, 0:T]), r=["lrr2", "rtmp"], w=["lrr2"])
        A("vector", lambda v: v.tensor_scalar(out=rtmp[0:1, 0:T], in0=lrr[1][:],
          scalar1=float(CL), scalar2=None, op0=ALU.is_lt), r=["lrr1"], w=["rtmp"])
        A("vector", lambda v: v.tensor_mul(out=lrr[3][:], in0=lrr[3][:],
          in1=rtmp[0:1, 0:T]), r=["lrr3", "rtmp"], w=["lrr3"])
        for k in range(2):
            srcrow = idxr1 if k == 0 else idxr2
            sk = _key(srcrow)
            A("vector", (lambda srcrow, k: lambda v: v.scalar_tensor_tensor(
                out=srcrow[:], in0=srcrow[:], scalar=float(CL),
                in1=lrr[k][:], op0=ALU.mult, op1=ALU.add))(srcrow, k),
                r=[sk, f"lrr{k}"], w=[sk])
            A("vector", (lambda srcrow: lambda v: v.tensor_scalar(
                out=srcrow[:], in0=srcrow[:], scalar1=float(SENT),
                scalar2=None, op0=ALU.subtract))(srcrow), r=[sk], w=[sk])
            A("vector", (lambda srcrow, k: lambda v: v.tensor_mul(
                out=srcrow[:], in0=srcrow[:],
                in1=lrr[2 + k][:]))(srcrow, k), r=[sk, f"lrr{2+k}"], w=[sk])
            A("vector", (lambda srcrow: lambda v: v.tensor_scalar(
                out=srcrow[:], in0=srcrow[:], scalar1=float(SENT),
                scalar2=None, op0=ALU.add))(srcrow), r=[sk], w=[sk])
        for i in range(NT):
            for k, srcrow in ((0, idxr1), (1, idxr2)):
                A("tensor", (lambda i, srcrow, k: lambda t: t.transpose(
                    out=ps_t[0:128, k:k + 1], in_=srcrow[0:1, i * 128:(i + 1) * 128],
                    identity=identf[0:1, 0:1]))(i, srcrow, k),
                    r=[_key(srcrow), "identf_s"], w=["ps_tf"])
            A("vector", lambda v: v.tensor_copy(out=slotc[:, 0:2], in_=ps_t[0:128, 0:2]),
              r=["ps_tf"], w=["slotc"])
            for k in range(2):
                A("tensor", (lambda i, k: lambda t: t.transpose(
                    out=ps_t[0:128, 2 + k:3 + k],
                    in_=lrr[2 + k][0:1, i * 128:(i + 1) * 128],
                    identity=identf[0:1, 0:1]))(i, k),
                    r=[f"lrr{2+k}", "identf_s"], w=["ps_tf2"])
            A("vector", lambda v: v.tensor_copy(out=slotc[:, 2:4], in_=ps_t[0:128, 2:4]),
              r=["ps_tf2"], w=["slotc"])
            A("vector", lambda v: v.tensor_copy(out=sloti[:, 0:2], in_=slotc[:, 0:2]),
              r=["slotc"], w=["sloti"])
            A("vector", (lambda i: lambda v: v.tensor_mul(
                out=sel[:, i * 8 + 2:i * 8 + 3], in0=sel[:, i * 8 + 2:i * 8 + 3],
                in1=slotc[:, 2:3]))(i), r=["sel", "slotc"], w=["sel"])
            A("vector", (lambda i: lambda v: v.tensor_mul(
                out=sel[:, i * 8 + 3:i * 8 + 4], in0=sel[:, i * 8 + 3:i * 8 + 4],
                in1=slotc[:, 3:4]))(i), r=["sel", "slotc"], w=["sel"])
            A("vector", (lambda i: lambda v: v.tensor_copy(
                out=tokstg[:], in_=tokid[:, i:i + 1]))(i), r=["tokid_s"], w=["tokstg"])
            for k in range(2):
                A("vector", (lambda k: lambda v: v.tensor_copy(
                    out=idxstg[:], in_=sloti[:, k:k + 1]))(k), r=["sloti"], w=["idxstg"])
                A("gpsimd", (lambda i, k: lambda g: g.indirect_dma_start(
                    out=inv_d, out_offset=bass.IndirectOffsetOnAxis(
                        ap=idxstg[:], axis=0),
                    in_=tokstg[:], in_offset=None))(i, k),
                    r=["idxstg", "tokstg"], w=["inv"], dma=True)

        if debug:
            A("sync", lambda s_: s_.dma_start(out=dbgm_d[0:1, 0:T], in_=idxr1[:]),
              r=["idxr1"], dma=True)
            A("sync", lambda s_: s_.dma_start(out=dbgm_d[1:2, 0:T], in_=idxr2[:]),
              r=["idxr2"], dma=True)
            for j in range(4):
                A("sync", (lambda j: lambda s_: s_.dma_start(
                    out=dbgm_d[2 + j:3 + j, 0:T], in_=lrr[j][:]))(j),
                    r=[f"lrr{j}"], dma=True)
            A("sync", lambda s_: s_.dma_start(out=dbgm_d[0:NE, 4096:4112], in_=counts[:]),
              r=["counts"], dma=True)
            A("sync", lambda s_: s_.dma_start(out=dbgm_d[:, 4112:4112 + 128], in_=sel[:]),
              r=["sel"], dma=True)
        inv_t = inv_d
        oub_t = oub_d.rearrange("(r p) d -> r p d", p=128)
        for ee in range(NE):
            for kk_ in range(8):
                A("sync", (lambda ee, kk_: lambda s_: s_.dma_start(
                    out=wstg[:], in_=w1_d.rearrange("(e k p) f -> e k p f", k=8, p=128)[ee][kk_]))(ee, kk_),
                    w=["wstg"], dma=True)
                A("vector", (lambda kk_: lambda v: v.tensor_copy(
                    out=w1b[:, kk_ * 2048:(kk_ + 1) * 2048], in_=wstg[:]))(kk_),
                    r=["wstg"], w=["w1b"])
            for kk_ in range(16):
                A("sync", (lambda ee, kk_: lambda s_: s_.dma_start(
                    out=wstg[:, 0:D], in_=w2_d.rearrange("(e k p) f -> e k p f", k=16, p=128)[ee][kk_]))(ee, kk_),
                    w=["wstg"], dma=True)
                A("vector", (lambda kk_: lambda v: v.tensor_copy(
                    out=w2b[:, kk_ * D:(kk_ + 1) * D], in_=wstg[:, 0:D]))(kk_),
                    r=["wstg"], w=["w2b"])
            for pp in range(3):
                base = ee * CL + pp * 128
                A("sync", (lambda base: lambda s_: s_.dma_start(
                    out=slotc[:, 0:1],
                    in_=inv_d[base:base + 128, :]))(base),
                    deps=[sch.last_w.get("inv")], w=["slotc"], dma=True)
                A("vector", lambda v: v.tensor_copy(out=idxstg[:],
                  in_=slotc[:, 0:1]), r=["slotc"], w=["idxstg"])
                A("gpsimd", lambda g: g.indirect_dma_start(
                    out=xg[:], out_offset=None, in_=xnb_d,
                    in_offset=bass.IndirectOffsetOnAxis(ap=idxstg[:], axis=0)),
                    r=["idxstg"] + [f"xnb{i}" for i in range(NT)], w=["xg"], dma=True)
                if debug and ee == 0 and pp == 0:
                    A("vector", lambda v: v.tensor_copy(out=g1[:], in_=xg[:]),
                      r=["xg"], w=["g1"])
                    A("sync", lambda s_: s_.dma_start(out=dbgm_d[:, 4240:5264], in_=g1[:]),
                      r=["g1"], dma=True)
                    A("sync", lambda s_: s_.dma_start(out=dbgm_d[:, 5264:5265], in_=slotc[:, 0:1],
                      allow_slow_non_contiguous=True), r=["slotc"], dma=True)
                for c in range(8):
                    emit_transpose(sch, xg[:, c * 128:(c + 1) * 128], "xg",
                                   xgT[:, c * 128:(c + 1) * 128], "xgT", ps_tb, identb,
                                   ps_key="ps_tbb", ident_key="identb_s")
                for n in range(4):
                    for c in range(8):
                        A("tensor", (lambda n, c: lambda t: t.matmul(
                            out=ps_m[:], lhsT=xgT[:, c * 128:(c + 1) * 128],
                            rhs=w1b[:, c * 2048 + n * 512:c * 2048 + (n + 1) * 512],
                            start=(c == 0), stop=(c == 7)))(n, c),
                            r=["xgT", "w1b"], w=["ps_m"])
                    A("scalar", (lambda n: lambda s_: s_.activation(
                        out=h1b[:, n * 512:(n + 1) * 512], in_=ps_m[:],
                        func=AF.Gelu))(n), r=["ps_m"], w=["h1b"])
                for c in range(16):
                    emit_transpose(sch, h1b[:, c * 128:(c + 1) * 128], "h1b",
                                   h1T[:, c * 128:(c + 1) * 128], "h1T", ps_tb, identb,
                                   ps_key="ps_tbb", ident_key="identb_s")
                for n in range(2):
                    for c in range(16):
                        A("tensor", (lambda n, c: lambda t: t.matmul(
                            out=ps_n[:], lhsT=h1T[:, c * 128:(c + 1) * 128],
                            rhs=w2b[:, c * D + n * 512:c * D + (n + 1) * 512],
                            start=(c == 0), stop=(c == 15)))(n, c),
                            r=["h1T", "w2b"], w=["ps_n"])
                    A("vector", (lambda n: lambda v: v.tensor_copy(
                        out=opc[:, n * 512:(n + 1) * 512], in_=ps_n[:]))(n),
                        r=["ps_n"], w=["opc"])
                if debug and ee == 0 and pp == 0:
                    A("sync", lambda s_: s_.dma_start(out=dbgm_d[:, 5266:6290], in_=opc[:]),
                      r=["opc"], dma=True)
                A("sync", (lambda ee, pp: lambda s_: s_.dma_start(
                    out=oub_t[ee * 3 + pp], in_=opc[:]))(ee, pp),
                    r=["opc"], w=["oub"], dma=True)

        for i in range(NT):
            A("sync", (lambda i: lambda s_: s_.dma_start(
                out=xtile[:], in_=xcp_t[i]))(i),
                deps=[sch.last_w.get(f"xcp{i}")], w=["xtile"], dma=True)
            for k in range(2):
                srcrow = idxr1 if k == 0 else idxr2
                A("tensor", (lambda i, srcrow: lambda t: t.transpose(
                    out=ps_t[0:128, 0:1], in_=srcrow[0:1, i * 128:(i + 1) * 128],
                    identity=identf[0:1, 0:1]))(i, srcrow),
                    r=[_key(srcrow), "identf_s"], w=["ps_tf"])
                A("vector", lambda v: v.tensor_copy(out=slotc[:, 0:1],
                  in_=ps_t[0:128, 0:1]), r=["ps_tf"], w=["slotc"])
                A("vector", lambda v: v.tensor_copy(out=idxstg[:],
                  in_=slotc[:, 0:1]), r=["slotc"], w=["idxstg"])
                A("gpsimd", (lambda k: lambda g: g.indirect_dma_start(
                    out=g1[:], out_offset=None, in_=oub_d,
                    in_offset=bass.IndirectOffsetOnAxis(ap=idxstg[:], axis=0)))(k),
                    r=["idxstg", "oub", "oub_sent"], w=["g1"], dma=True)
                A("vector", (lambda i, k: lambda v: v.scalar_tensor_tensor(
                    out=xtile[:], in0=g1[:], scalar=sel[:, i * 8 + 2 + k:i * 8 + 3 + k],
                    in1=xtile[:], op0=ALU.mult, op1=ALU.add))(i, k),
                    r=["g1", "sel", "xtile"], w=["xtile"])
            A("vector", lambda v: v.tensor_scalar(out=outt[:], in0=xtile[:],
              scalar1=1.0, scalar2=None, op0=ALU.mult), r=["xtile"], w=["outt"])
            A("sync", (lambda i: lambda s_: s_.dma_start(
                out=xo_d.rearrange("(m p) d -> m p d", p=128)[i], in_=outt[:]))(i),
                r=["outt"], w=[f"xo{i}"], dma=True)

        make_block(nc, ctx, sch, final_eng="gpsimd")
    return nc


_MOE_CACHE = {}


def run_moe(x_full, gate_w, w1, w2):
    import os
    dbg = bool(int(os.environ.get("MOE_DEBUG", "0")))
    if "nc" not in _MOE_CACHE:
        _MOE_CACHE["nc"] = build_moe(debug=dbg)
    nc = _MOE_CACHE["nc"]
    shards = np.split(_f32(x_full), NCORE, axis=0)
    gwT = _f32(np.asarray(gate_w).T)
    w1f = _f32(np.asarray(w1).reshape(NE * D, 2048))
    w2f = _f32(np.asarray(w2).reshape(NE * 2048, D))
    iotaE = _f32(np.broadcast_to(np.arange(NE, dtype=np.float64)[None, :], (128, NE)))
    iotaEc = _f32(np.arange(NE, dtype=np.float64)[:, None])
    tokid = _f32((np.arange(128)[:, None] + 128 * np.arange(NT)[None, :]))
    in_maps = []
    for c in range(NCORE):
        prevm = np.zeros((NE, 8), np.float32)
        prevm[:, :c] = 1.0
        m = {"x": shards[c], "gwT": gwT, "w1": w1f, "w2": w2f,
             "identb": _bf16(np.eye(128)), "identf": _f32(np.eye(128)),
             "iotaE": iotaE, "iotaEc": iotaEc, "tokid": tokid, "prevm": prevm}
        in_maps.append(m)
    res = run_bass_kernel_spmd(nc, in_maps, core_ids=list(range(NCORE)))
    if dbg:
        np.save("/tmp/moe_dbg.npy", res.results[0]["dbgm"])
    return np.concatenate([r["xo"] for r in res.results], axis=0)
